# version 04
# baseline (speedup 1.0000x reference)
import numpy as np
import ml_dtypes
from contextlib import ExitStack

import concourse.bass as bass
import concourse.mybir as mybir
from concourse.bass_utils import run_bass_kernel_spmd

F32 = mybir.dt.float32
BF16 = mybir.dt.bfloat16
AF = mybir.ActivationFunctionType
ALU = mybir.AluOpType
AX = mybir.AxisListType

S, D, NH, DH, CH = 4096, 2048, 16, 128, 256
NCH = S // CH
H = 1408
NEXP = 16
KD, KH = D // 128, H // 128
SB = S // 128
STRIPE = 1024
NSTRIPE = S // STRIPE
SBS = STRIPE // 128
EPS = 1e-5
ARENA = 103000

C_EPS, C_NEGHALF, C_INVD, C_ONE, C_EPS20, C_NEG1 = 0, 1, 2, 3, 4, 5
NCONST = 8


def _mk_consts():
    v = np.zeros((128, NCONST), np.float32)
    v[:, C_EPS] = EPS
    v[:, C_NEGHALF] = -0.5
    v[:, C_INVD] = 1.0 / D
    v[:, C_ONE] = 1.0
    v[:, C_EPS20] = 1e-20
    v[:, C_NEG1] = -1.0
    return v


def _rope_tables():
    inv = 1.0 / (10000.0 ** (np.arange(0, DH, 2, dtype=np.float32) / DH))
    pos = np.arange(S, dtype=np.float32)
    emb = np.concatenate([np.outer(pos, inv), np.outer(pos, inv)], axis=1)
    cos = np.cos(emb).T.astype(np.float32)
    sin = np.sin(emb).T.astype(np.float32)
    sgn = np.where(np.arange(DH) % 2 == 0, -1.0, 1.0).astype(np.float32)[:, None]
    return (np.ascontiguousarray(cos.astype(ml_dtypes.bfloat16)),
            np.ascontiguousarray((sin * sgn).astype(ml_dtypes.bfloat16)))


class Prog:
    ENGS = ("sync", "gpsimd", "scalar", "vector", "tensor")

    def __init__(self):
        self.ops = {e: [] for e in self.ENGS}
        self.cnt = {}
        self.bank_guard = [None] * 8

    def sem(self, name):
        self.cnt[name] = 0

    def bump(self, s, k):
        self.cnt[s] += k
        return self.cnt[s]

    def wait(self, eng, sem, val=None):
        v = self.cnt[sem] if val is None else val
        if v <= 0:
            return
        self.ops[eng].append(lambda h, env, s=sem, vv=v: h.wait_ge(env[s], vv))

    def op(self, eng, f, inc=None, inck=1):
        if inc is not None:
            self.bump(inc, inck)

            def g(h, env, f=f, s=inc, k=inck):
                f(h, env).then_inc(env[s], k)
            self.ops[eng].append(g)
        else:
            self.ops[eng].append(lambda h, env, f=f: f(h, env))

    def dma(self, eng, out, in_, inc="d_g"):
        self.op(eng, lambda h, env, o=out, i=in_: h.dma_start(out=o, in_=i),
                inc=inc, inck=16)

    def mm(self, out, lhsT, rhs, start, stop, inc=None):
        self.op("tensor",
                lambda h, env, o=out, l=lhsT, r=rhs, a=start, b=stop:
                h.matmul(o, l, r, start=a, stop=b), inc=inc)

    def tr(self, out, in_, ident, inc=None):
        self.op("tensor",
                lambda h, env, o=out, i=in_, d=ident: h.transpose(o, i, d),
                inc=inc)

    def acopy(self, out, in_, inc=None):
        self.op("scalar", lambda h, env, o=out, i=in_: h.copy(o, i), inc=inc)

    def act(self, out, in_, func, bias=0.0, scale=1.0, accum=None, inc=None):
        self.op("scalar",
                lambda h, env, o=out, i=in_, f=func, b=bias, s=scale, a=accum:
                h.activation(o, i, f, bias=b, scale=s, accum_out=a), inc=inc)

    def vcopy(self, out, in_, inc=None):
        self.op("vector", lambda h, env, o=out, i=in_: h.tensor_copy(o, i),
                inc=inc)

    def tt(self, out, a, b, alu, inc=None):
        self.op("vector",
                lambda h, env, o=out, x=a, y=b, u=alu:
                h.tensor_tensor(o, x, y, u), inc=inc)

    def ts(self, out, a, s1, op0, s2=None, op1=None, inc=None):
        op1 = ALU.bypass if op1 is None else op1
        self.op("vector",
                lambda h, env, o=out, x=a, u=s1, v=s2, p0=op0, p1=op1:
                h.tensor_scalar(o, x, u, v, p0, p1), inc=inc)

    def stt(self, out, a, scal, b, op0, op1, inc=None):
        self.op("vector",
                lambda h, env, o=out, x=a, s=scal, y=b, p0=op0, p1=op1:
                h.scalar_tensor_tensor(o, x, s, y, p0, p1), inc=inc)

    def rsum(self, out, in_, inc=None):
        self.op("vector",
                lambda h, env, o=out, i=in_: h.reduce_sum(o, i, axis=AX.X),
                inc=inc)

    def rmax(self, out, in_, inc=None):
        self.op("vector",
                lambda h, env, o=out, i=in_: h.reduce_max(o, i, axis=AX.X),
                inc=inc)

    def recip(self, out, in_, inc=None):
        self.op("vector", lambda h, env, o=out, i=in_: h.reciprocal(o, i),
                inc=inc)

    def memset(self, eng, ap_, val, inc=None):
        self.op(eng, lambda h, env, a=ap_, v=val: h.memset(a, v), inc=inc)

    def vfence(self):
        self.wait("vector", "m_dve")

    def afence(self):
        self.wait("scalar", "m_act")

    def pe_bank(self, b):
        g = self.bank_guard[b]
        if g is not None:
            self.wait("tensor", g[0], g[1])
            self.bank_guard[b] = None

    def bank_read(self, b, sem):
        self.bank_guard[b] = (sem, self.cnt[sem])

    def build(self, block, env):
        def mk(ops):
            def body(h):
                for f in ops:
                    f(h, env)
            return body
        block.sync(mk(self.ops["sync"]))
        block.gpsimd(mk(self.ops["gpsimd"]))
        block.scalar(mk(self.ops["scalar"]))
        block.vector(mk(self.ops["vector"]))
        block.tensor(mk(self.ops["tensor"]))


def build_program(stage=4, dbg4=False):
    nc = bass.Bass("TRN2", target_bir_lowering=False, debug=False)

    x_in = nc.dram_tensor("x", [S, D], F32, kind="ExternalInput").ap()
    Wqkv = nc.dram_tensor("W_qkv", [3 * D, D], F32, kind="ExternalInput").ap()
    Wout = nc.dram_tensor("W_out", [D, D], F32, kind="ExternalInput").ap()
    gw = nc.dram_tensor("gate_w", [NEXP, D], F32, kind="ExternalInput").ap()
    w1 = nc.dram_tensor("w1", [NEXP, D, H], F32, kind="ExternalInput").ap()
    w2 = nc.dram_tensor("w2", [NEXP, H, D], F32, kind="ExternalInput").ap()
    sw1 = nc.dram_tensor("sw1", [1, D, H], F32, kind="ExternalInput").ap()
    sw2 = nc.dram_tensor("sw2", [1, H, D], F32, kind="ExternalInput").ap()

    def scr(name, shape, dtype, dbg):
        kind = "ExternalOutput" if stage == dbg else "Internal"
        return nc.dram_tensor(name, shape, dtype, kind=kind).ap()

    qd_s = scr("qd_s", [NH, 128, S], BF16, 1)
    kd_s = scr("kd_s", [NH, 128, S], BF16, 1)
    vd_s = scr("vd_s", [NH, 128, S], BF16, 1)
    ao_s = scr("ao_s", [NH, 128, S], BF16, 2)
    x1f_s = scr("x1f_s", [SB, 128, D], F32, 3)
    x1t_s = scr("x1t_s", [2 * NSTRIPE, 128, KD * 512], BF16, 3)
    out_t = (nc.dram_tensor("out", [S, D], F32, kind="ExternalOutput").ap()
             if stage >= 4 else None)
    den_dbg = (nc.dram_tensor("den_dbg", [NH, 128, SB], F32,
                              kind="ExternalOutput").ap() if stage == 2 else None)
    kv_dbg = (nc.dram_tensor("kv_dbg", [2, 128, SB * 129], BF16,
                             kind="ExternalOutput").ap() if stage == 2 else None)
    ss_dbg = (nc.dram_tensor("ss_dbg", [128, 512 + 256 + 129], F32,
                             kind="ExternalOutput").ap() if stage == 2 else None)
    g_dbg = (nc.dram_tensor("g_dbg", [3, 128, 128], F32,
                            kind="ExternalOutput").ap() if dbg4 else None)
    y_dbg = (nc.dram_tensor("y_dbg", [128, D], F32,
                            kind="ExternalOutput").ap() if dbg4 else None)

    cos_np, sins_np = _rope_tables()
    maskT_np = (np.arange(384)[None, :] >= (np.arange(128)[:, None] + 128)
                ).astype(np.float32)
    pswap_np = np.zeros((128, 128), dtype=ml_dtypes.bfloat16)
    for i in range(0, 128, 2):
        pswap_np[i, i + 1] = 1.0
        pswap_np[i + 1, i] = 1.0
    c_cos = nc.inline_tensor(cos_np, "c_cos").ap()
    c_sins = nc.inline_tensor(sins_np, "c_sins").ap()
    c_mask = nc.inline_tensor(maskT_np, "c_mask").ap()
    c_psw = nc.inline_tensor(pswap_np, "c_psw").ap()
    c_id = nc.inline_tensor(np.eye(128, dtype=ml_dtypes.bfloat16), "c_id").ap()
    c_cc = nc.inline_tensor(_mk_consts(), "c_cc").ap()

    ctx = ExitStack()
    with ctx:
        arena = ctx.enter_context(nc.sbuf_tensor("arena", [128, ARENA], BF16))

        def bfv(off, n):
            return arena[:, off:off + n]

        def f32v(off, n):
            return arena[:, off:off + 2 * n].bitcast(F32)

        ps = [ctx.enter_context(nc.psum_tensor(f"ps{i}", [128, 512], F32))
              for i in range(8)]

        def ps_bf(b, t):
            return ps[b][:, t * 64:(t + 1) * 64].bitcast(BF16)

        def ps_bf8(b):
            return ps[b][:, 0:512].bitcast(BF16).rearrange(
                "p (t d) -> p t d", d=128)

        p = Prog()
        env = {}
        for sname in ("d_g", "d_s", "m_pe", "m_act", "m_dve", "m_pool"):
            p.sem(sname)
            env[sname] = ctx.enter_context(nc.semaphore(sname))
        block = ctx.enter_context(nc.Block())

        M = {}

        def mark(name, sem):
            M[name] = (sem, p.cnt[sem])

        def wm(eng, name):
            s_, v_ = M[name]
            p.wait(eng, s_, v_)

        A_COS, A_SINS, A_MASK, A_PSW, A_ID = 0, 4096, 8192, 8960, 9088
        A_GATET, A_CCOL = 102400, 102800
        cosv, sinsv = bfv(A_COS, S), bfv(A_SINS, S)
        maskv = f32v(A_MASK, 384)
        pswv, idv = bfv(A_PSW, 128), bfv(A_ID, 128)
        gateTv3 = bfv(A_GATET, KD * NEXP).rearrange("p (k e) -> p k e", e=NEXP)
        ccol = f32v(A_CCOL, NCONST)

        def cc(j):
            return ccol[:, j:j + 1]

        p.dma("gpsimd", cosv, c_cos)
        p.dma("gpsimd", sinsv, c_sins)
        p.dma("gpsimd", maskv, c_mask)
        p.dma("gpsimd", pswv, c_psw)
        p.dma("gpsimd", idv, c_id)
        p.dma("gpsimd", ccol, c_cc)
        A_GWB = 9600
        gwb = bfv(A_GWB, D)
        p.dma("gpsimd", gwb[0:NEXP, :], gw)
        mark("p0_loads", "d_g")
        wm("tensor", "p0_loads")
        for half in range(2):
            bank = 6 + half
            p.pe_bank(bank)
            for t in range(8):
                k = half * 8 + t
                p.tr(ps_bf(bank, t)[:, 0:NEXP],
                     gwb[0:NEXP, k * 128:(k + 1) * 128],
                     idv[0:NEXP, 0:NEXP], inc="m_pe" if t == 7 else None)
            p.wait("scalar", "m_pe")
            p.acopy(gateTv3[:, half * 8:half * 8 + 8, :],
                    ps_bf8(bank)[:, :, 0:NEXP], inc="m_act")
            p.bank_read(bank, "m_act")
        mark("gateT", "m_act")

        A_XT, A_WB, A_WT, A_XB, A_QPL = 9600, 75136, 79232, 83328, 87424
        A_R1, A_R2, A_R3 = 95616, 97664, 99712
        xTv = [bfv(A_XT + k * S, S) for k in range(KD)]
        xT8 = [bfv(A_XT + h8 * 8 * S, 8 * S).rearrange("p (t s) -> p t s", s=S)
               for h8 in range(2)]
        wbv = [bfv(A_WB + i * D, D) for i in range(2)]
        wTv = [bfv(A_WT + i * D, D) for i in range(2)]
        xbv = [bfv(A_XB + i * D, D) for i in range(2)]
        qplv = [bfv(A_QPL + i * S, S) for i in range(2)]
        r1p = [f32v(A_R1 + i * 1024, 512) for i in range(2)]
        r2p = [f32v(A_R2 + i * 1024, 512) for i in range(2)]
        r3p = [f32v(A_R3 + i * 1024, 512) for i in range(2)]

        for sbi in range(SB):
            if sbi >= 2:
                wm("gpsimd", f"p1a_tr{sbi - 2}")
            p.dma("gpsimd", xbv[sbi % 2], x_in[sbi * 128:(sbi + 1) * 128, :])
            xl = p.cnt["d_g"]
            p.wait("tensor", "d_g", xl)
            for half in range(2):
                bank = 6 + half
                p.pe_bank(bank)
                for t in range(8):
                    k = half * 8 + t
                    p.tr(ps_bf(bank, t),
                         xbv[sbi % 2][:, k * 128:(k + 1) * 128], idv,
                         inc="m_pe" if t == 7 else None)
                if half == 1:
                    mark(f"p1a_tr{sbi}", "m_pe")
                p.wait("scalar", "m_pe")
                p.acopy(xT8[half][:, :, sbi * 128:sbi * 128 + 128],
                        ps_bf8(bank), inc="m_act")
                p.bank_read(bank, "m_act")
        mark("p1a_done", "m_act")

        GROUPS = [(0, 1, 2), (3, 4, 5), (6, 7)]
        GBANKS = [(0, 1, 2), (3, 4, 5), (0, 1)]
        wload = []
        p.dma("gpsimd", wbv[0], Wqkv[0:128, :])
        wload.append(p.cnt["d_g"])
        p.dma("gpsimd", wbv[1], Wqkv[128:256, :])
        wload.append(p.cnt["d_g"])
        wm("tensor", "p1a_done")

        for jb in range(48):
            which, head = jb // NH, jb % NH
            buf = jb % 2
            p.wait("tensor", "d_g", wload[jb])
            for half in range(2):
                bank = 6 + half
                p.pe_bank(bank)
                for t in range(8):
                    k = half * 8 + t
                    p.tr(ps_bf(bank, t),
                         wbv[buf][:, k * 128:(k + 1) * 128], idv,
                         inc="m_pe" if t == 7 else None)
                if half == 1:
                    mark(f"wtr{jb}", "m_pe")
                p.wait("scalar", "m_pe")
                p.acopy(bfv(A_WT + buf * D + half * 1024, 1024).rearrange(
                    "p (t d) -> p t d", d=128), ps_bf8(bank), inc="m_act")
                p.bank_read(bank, "m_act")
            mark(f"wtrev{jb}", "m_act")
            if jb + 2 < 48:
                wm("gpsimd", f"wtr{jb}")
                p.dma("gpsimd", wbv[buf], Wqkv[(jb + 2) * 128:(jb + 3) * 128, :])
                wload.append(p.cnt["d_g"])
            if jb >= 2:
                wm("scalar", f"spill{jb - 2}")
            wm("tensor", f"wtrev{jb}")
            for gi, grp in enumerate(GROUPS):
                banks = GBANKS[gi]
                for b in banks:
                    p.pe_bank(b)
                for k in range(KD):
                    lhsT = wTv[buf][:, k * 128:(k + 1) * 128]
                    for ci, c in enumerate(grp):
                        p.mm(ps[banks[ci]][:, :], lhsT,
                             xTv[k][:, c * 512:(c + 1) * 512],
                             start=(k == 0), stop=(k == KD - 1),
                             inc=("m_pe" if (k == KD - 1 and
                                             ci == len(grp) - 1) else None))
                mark(f"qkvmm{jb}_{gi}", "m_pe")
                wm("scalar", f"qkvmm{jb}_{gi}")
                for ci, c in enumerate(grp):
                    p.acopy(qplv[buf][:, c * 512:(c + 1) * 512],
                            ps[banks[ci]][:, :], inc="m_act")
                    p.bank_read(banks[ci], "m_act")
                    mark(f"qkvev{jb}_{c}", "m_act")
            if which < 2:
                for c in range(8):
                    par = c % 2
                    bank = 6 + par
                    cs = slice(c * 512, (c + 1) * 512)
                    p.pe_bank(bank)
                    wm("tensor", f"qkvev{jb}_{c}")
                    p.mm(ps[bank][:, :], pswv, qplv[buf][:, cs],
                         start=True, stop=True, inc="m_pe")
                    mark(f"psw{jb}_{c}", "m_pe")
                    if c >= 2:
                        wm("vector", f"rope2_{jb}_{c - 2}")
                    wm("vector", f"psw{jb}_{c}")
                    p.tt(r2p[par], ps[bank][:, :], sinsv[:, cs], ALU.mult,
                         inc="m_dve")
                    p.bank_read(bank, "m_dve")
                    p.tt(r1p[par], qplv[buf][:, cs], cosv[:, cs], ALU.mult)
                    p.tt(r1p[par], r1p[par], r2p[par], ALU.add, inc="m_dve")
                    mark(f"rope1_{jb}_{c}", "m_dve")
                    wm("scalar", f"rope1_{jb}_{c}")
                    p.act(r2p[par], r1p[par], AF.Exp)
                    p.act(r3p[par], r1p[par], AF.Relu, inc="m_act")
                    mark(f"rope2_{jb}_{c}", "m_act")
                    wm("vector", f"rope2_{jb}_{c}")
                    p.ts(r2p[par], r2p[par], 1.0, ALU.min)
                    p.tt(r1p[par], r2p[par], r3p[par], ALU.add)
                    p.vcopy(qplv[buf][:, cs], r1p[par], inc="m_dve")
                    mark(f"ropewb{jb}_{c}", "m_dve")
                wm("sync", f"ropewb{jb}_{7}")
                p.dma("sync", (qd_s if which == 0 else kd_s)[head],
                      qplv[buf], inc="d_s")
            else:
                wm("sync", f"qkvev{jb}_{7}")
                p.dma("sync", vd_s[head], qplv[buf], inc="d_s")
            mark(f"spill{jb}", "d_s")
        mark("p1_pe_done", "m_pe")
        mark("p1_act_done", "m_act")
        mark("p1_dve_done", "m_dve")
        if stage == 1:
            _finish(p, env, block)
            return nc

        A_QH, A_KH, A_VH = 9600, 17792, 25984
        A_KS, A_VE = 34176, 38272
        A_ATT, A_SSB = 42400, 43424
        A_NUMF, A_DEN, A_DENR = 43568, 51760, 51824
        A_AOB, A_AOPL = 51888, 52912
        A_WOT, A_WOB = 61104, 93872
        qhv = [bfv(A_QH + i * S, S) for i in range(2)]
        khv = [bfv(A_KH + i * S, S) for i in range(2)]
        vhv = [bfv(A_VH + i * S, S) for i in range(2)]
        ksv3 = bfv(A_KS, SB * 128).rearrange("p (t d) -> p t d", d=128)
        vev_flat = bfv(A_VE, SB * 129)
        vev3 = vev_flat.rearrange("p (t d) -> p t d", d=129)
        attv2 = [bfv(A_ATT, 512), bfv(A_ATT + 512, 512)]
        ssbv = bfv(A_SSB, 129)
        numfv = f32v(A_NUMF, SB * 128)
        denv, denrv = f32v(A_DEN, SB), f32v(A_DENR, SB)
        aobv = bfv(A_AOB, 1024)
        aoplv = [bfv(A_AOPL + i * S, S) for i in range(2)]
        woTv = [bfv(A_WOT + k * D, D) for k in range(KD)]
        wobv = [bfv(A_WOB + i * D, D) for i in range(2)]

        wm("gpsimd", "p1_pe_done")
        wm("gpsimd", "p1_act_done")
        wm("gpsimd", "p1_dve_done")
        wm("gpsimd", f"spill{47}")
        p.memset("gpsimd", vev_flat, 1.0, inc="m_pool")
        mark("ve_ones", "m_pool")

        hload = []
        for hh in range(2):
            for src, dstb in ((qd_s, qhv), (kd_s, khv), (vd_s, vhv)):
                p.dma("gpsimd", dstb[hh], src[hh])
            hload.append(p.cnt["d_g"])

        for h_i in range(NH):
            buf = h_i % 2
            p.wait("tensor", "d_g", hload[h_i])
            for src, kind in ((khv[buf], "ks"), (vhv[buf], "ve")):
                for batch in range(4):
                    bank = 6 + (batch % 2)
                    p.pe_bank(bank)
                    for t in range(8):
                        tt_ = batch * 8 + t
                        p.tr(ps_bf(bank, t),
                             src[:, tt_ * 128:(tt_ + 1) * 128], idv,
                             inc="m_pe" if t == 7 else None)
                    p.wait("scalar", "m_pe")
                    if kind == "ve":
                        wm("scalar", "ve_ones")
                        dest = vev3[:, batch * 8:batch * 8 + 8, 0:128]
                    else:
                        dest = ksv3[:, batch * 8:batch * 8 + 8, :]
                    p.acopy(dest, ps_bf8(bank), inc="m_act")
                    p.bank_read(bank, "m_act")
            mark(f"kv_tr{h_i}", "m_act")
            if stage == 2 and h_i == 0:
                wm("sync", f"kv_tr{h_i}")
                p.dma("sync", kv_dbg[0][:, 0:SB * 128],
                      bfv(A_KS, SB * 128), inc="d_s")
                p.dma("sync", kv_dbg[1], vev_flat, inc="d_s")
                mark("kvdbg", "d_s")
            if h_i > 0:
                wm("vector", f"aocast{h_i - 1}")
                wm("tensor", f"amask{h_i - 1}_{NCH - 1}")
                wm("tensor", f"numev{h_i - 1}_{NCH - 1}")
                wm("tensor", f"scopy{h_i - 1}_{NCH - 1}")
            wm("tensor", f"kv_tr{h_i}")
            for b in (1, 2, 3):
                p.pe_bank(b)

            def att_group(c, h_i=h_i, buf=buf):
                bank = 0 if c % 2 == 0 else 4
                s0 = c * CH
                p.pe_bank(bank)
                p.mm(ps[bank][:, 0:256], khv[buf][:, s0:s0 + 128],
                     qhv[buf][:, s0:s0 + 256], start=True, stop=True)
                p.mm(ps[bank][:, 256:512], khv[buf][:, s0 + 128:s0 + 256],
                     qhv[buf][:, s0:s0 + 256], start=True, stop=True,
                     inc="m_pe")
                mark(f"att{h_i}_{c}", "m_pe")
                wm("vector", f"att{h_i}_{c}")
                if c >= 2:
                    wm("vector", f"num{h_i}_{c - 2}")
                av = attv2[c % 2]
                p.tt(av[:, 0:256], ps[bank][:, 0:256], maskv[:, 128:384],
                     ALU.mult)
                p.tt(av[:, 256:512], ps[bank][:, 256:512], maskv[:, 0:256],
                     ALU.mult, inc="m_dve")
                p.bank_read(bank, "m_dve")
                mark(f"amask{h_i}_{c}", "m_dve")

            def num_group(c, h_i=h_i, buf=buf):
                s0 = c * CH
                av = attv2[c % 2]
                wm("tensor", f"amask{h_i}_{c}")
                if c > 0:
                    wm("tensor", f"scopy{h_i}_{c}")
                    wm("tensor", f"numev{h_i}_{c - 1}")
                for lh in range(2):
                    psn = ps[1 + lh]
                    lc = lh * 128
                    p.mm(psn[:, 0:129], av[:, lc:lc + 128],
                         vev3[:, 2 * c, :], start=True, stop=False)
                    if c == 0:
                        p.mm(psn[:, 0:129], av[:, 256 + lc:256 + lc + 128],
                             vev3[:, 2 * c + 1, :], start=False, stop=True,
                             inc=("m_pe" if lh == 1 else None))
                    else:
                        p.mm(psn[:, 0:129], av[:, 256 + lc:256 + lc + 128],
                             vev3[:, 2 * c + 1, :], start=False, stop=False)
                        p.mm(psn[:, 0:129], qhv[buf][:, s0 + lc:s0 + lc + 128],
                             ssbv[:, 0:129], start=False, stop=True,
                             inc=("m_pe" if lh == 1 else None))
                mark(f"num{h_i}_{c}", "m_pe")
                p.mm(ps[3][:, 0:129], ksv3[:, 2 * c, :], vev3[:, 2 * c, :],
                     start=(c == 0), stop=False)
                p.mm(ps[3][:, 0:129], ksv3[:, 2 * c + 1, :],
                     vev3[:, 2 * c + 1, :], start=False, stop=(c == NCH - 1),
                     inc="m_pe")
                mark(f"upd{h_i}_{c}", "m_pe")
                if c < NCH - 1:
                    wm("scalar", f"upd{h_i}_{c}")
                    p.acopy(ssbv[:, 0:129], ps[3][:, 0:129], inc="m_act")
                    mark(f"scopy{h_i}_{c + 1}", "m_act")
                wm("vector", f"num{h_i}_{c}")
                for lh in range(2):
                    i_ = 2 * c + lh
                    p.vcopy(numfv[:, i_ * 128:(i_ + 1) * 128],
                            ps[1 + lh][:, 0:128])
                    p.vcopy(denv[:, i_:i_ + 1], ps[1 + lh][:, 128:129],
                            inc="m_dve")
                mark(f"numev{h_i}_{c}", "m_dve")

            att_group(0)
            for c in range(NCH):
                if c + 1 < NCH:
                    att_group(c + 1)
                num_group(c)
            p.vfence()
            p.ts(denv[:, :], denv[:, :], 1e-6, ALU.max, inc="m_dve")
            p.vfence()
            p.recip(denrv[:, :], denv[:, :], inc="m_dve")
            p.vfence()
            for i_ in range(SB):
                p.ts(numfv[:, i_ * 128:(i_ + 1) * 128],
                     numfv[:, i_ * 128:(i_ + 1) * 128],
                     denrv[:, i_:i_ + 1], ALU.mult,
                     inc="m_dve" if i_ == SB - 1 else None)
            mark(f"aoscale{h_i}", "m_dve")
            if stage == 2 and den_dbg is not None:
                wm("sync", f"aoscale{h_i}")
                p.dma("sync", den_dbg[h_i], denv, inc="d_s")
            if h_i >= 2:
                wm("scalar", f"aospill{h_i - 2}")
            wm("scalar", f"aoscale{h_i}")
            for batch in range(4):
                bank = 6 + (batch % 2)
                p.acopy(aobv[:, :],
                        numfv[:, batch * 1024:(batch + 1) * 1024],
                        inc="m_act")
                mark(f"aocastb{h_i}_{batch}", "m_act")
                p.pe_bank(bank)
                wm("tensor", f"aocastb{h_i}_{batch}")
                for t in range(8):
                    p.tr(ps_bf(bank, t), aobv[:, t * 128:(t + 1) * 128], idv,
                         inc="m_pe" if t == 7 else None)
                p.wait("scalar", "m_pe")
                p.acopy(aoplv[buf][:, :].rearrange("p (t s) -> p t s", s=128)
                        [:, batch * 8:batch * 8 + 8, :],
                        ps_bf8(bank), inc="m_act")
                p.bank_read(bank, "m_act")
            mark(f"aocast{h_i}", "m_act")
            wm("sync", f"aocast{h_i}")
            p.dma("sync", ao_s[h_i], aoplv[buf], inc="d_s")
            mark(f"aospill{h_i}", "d_s")
            if h_i + 2 < NH:
                wm("gpsimd", f"num{h_i}_{NCH - 1}")
                for src, dstb in ((qd_s, qhv), (kd_s, khv), (vd_s, vhv)):
                    p.dma("gpsimd", dstb[buf], src[h_i + 2])
                hload.append(p.cnt["d_g"])
            if h_i >= 2:
                wm("gpsimd", f"wotr{h_i - 2}")
            p.dma("gpsimd", wobv[buf], Wout[h_i * 128:(h_i + 1) * 128, :])
            wol = p.cnt["d_g"]
            p.wait("tensor", "d_g", wol)
            for half in range(2):
                bank = 6 + half
                p.pe_bank(bank)
                for t in range(8):
                    k = half * 8 + t
                    p.tr(ps_bf(bank, t),
                         wobv[buf][:, k * 128:(k + 1) * 128], idv,
                         inc="m_pe" if t == 7 else None)
                if half == 1:
                    mark(f"wotr{h_i}", "m_pe")
                p.wait("scalar", "m_pe")
                p.acopy(bfv(A_WOT + half * 8 * D, 8 * D).rearrange(
                    "p (t s) -> p t s", s=D)
                    [:, :, h_i * 128:h_i * 128 + 128],
                    ps_bf8(bank), inc="m_act")
                p.bank_read(bank, "m_act")
        mark("p2_pe_done", "m_pe")
        mark("p2_act_done", "m_act")
        mark("p2_dve_done", "m_dve")
        if stage == 2:
            _finish(p, env, block)
            return nc

        A_AOT, A_XRES, A_TF = 9600, 13696, 21888
        A_SQF, A_X1B, A_X1T, A_ST = 25984, 30080, 32128, 36224
        aotv = [bfv(A_AOT + i * D, D).rearrange("p (k d) -> p k d", d=128)
                for i in range(2)]
        xresv = [f32v(A_XRES + i * 2 * D, D) for i in range(2)]
        tfv = f32v(A_TF, D)
        sqfv = f32v(A_SQF, D)
        x1bv = bfv(A_X1B, D)
        x1tv = [bfv(A_X1T + i * D, D) for i in range(2)]
        stv = f32v(A_ST, 16)
        ST_SUM, ST_SQ, ST_MEAN, ST_Q1, ST_M2, ST_VAR, ST_RSTD, ST_NMR = range(8)

        def st(j):
            return stv[:, j:j + 1]

        wm("gpsimd", "p2_pe_done")
        wm("gpsimd", "p2_act_done")
        wm("gpsimd", "p2_dve_done")
        wm("gpsimd", f"aospill{NH - 1}")
        aol = []
        for i in range(2):
            src = ao_s[:, :, i * 128:(i + 1) * 128].transpose([1, 0, 2])
            p.dma("gpsimd", aotv[i][:, 0:8, :], src[:, 0:8, :])
            p.dma("gpsimd", aotv[i][:, 8:16, :], src[:, 8:16, :])
            p.dma("gpsimd", xresv[i], x_in[i * 128:(i + 1) * 128, :])
            aol.append(p.cnt["d_g"])
        for sbi in range(SB):
            buf = sbi % 2
            p.wait("tensor", "d_g", aol[sbi])
            for b in range(4):
                p.pe_bank(b)
            for k in range(KD):
                lhsT = aotv[buf][:, k, :]
                for ec in range(4):
                    p.mm(ps[ec][:, :], lhsT,
                         woTv[k][:, ec * 512:(ec + 1) * 512],
                         start=(k == 0), stop=(k == KD - 1),
                         inc=("m_pe" if (k == KD - 1 and ec == 3) else None))
            mark(f"p3mm{sbi}", "m_pe")
            if sbi > 0:
                wm("vector", f"p3norm{sbi - 1}")
            wm("vector", f"p3mm{sbi}")
            for ec in range(4):
                p.tt(tfv[:, ec * 512:(ec + 1) * 512], ps[ec][:, :],
                     xresv[buf][:, ec * 512:(ec + 1) * 512], ALU.add,
                     inc="m_dve")
                p.bank_read(ec, "m_dve")
            p.rsum(st(ST_SUM), tfv[:, :], inc="m_dve")
            mark(f"p3t{sbi}", "m_dve")
            p.vfence()
            if sbi > 0:
                wm("scalar", f"p3spill{sbi - 1}")
            wm("scalar", f"p3t{sbi}")
            p.act(sqfv[:, :], tfv[:, :], AF.Square, accum=st(ST_SQ),
                  inc="m_act")
            mark(f"p3sq{sbi}", "m_act")
            wm("vector", f"p3sq{sbi}")
            p.ts(st(ST_MEAN), st(ST_SUM), cc(C_INVD), ALU.mult, inc="m_dve")
            p.ts(st(ST_Q1), st(ST_SQ), cc(C_INVD), ALU.mult, inc="m_dve")
            p.vfence()
            p.tt(st(ST_M2), st(ST_MEAN), st(ST_MEAN), ALU.mult, inc="m_dve")
            p.vfence()
            p.tt(st(ST_VAR), st(ST_Q1), st(ST_M2), ALU.subtract, inc="m_dve")
            mark(f"p3var{sbi}", "m_dve")
            wm("scalar", f"p3var{sbi}")
            p.act(st(ST_Q1), st(ST_VAR), AF.Ln, bias=cc(C_EPS), inc="m_act")
            p.afence()
            p.act(st(ST_RSTD), st(ST_Q1), AF.Exp, scale=cc(C_NEGHALF),
                  inc="m_act")
            mark(f"p3rstd{sbi}", "m_act")
            wm("vector", f"p3rstd{sbi}")
            p.tt(st(ST_NMR), st(ST_MEAN), st(ST_RSTD), ALU.mult, inc="m_dve")
            p.vfence()
            p.ts(st(ST_NMR), st(ST_NMR), -1.0, ALU.mult, inc="m_dve")
            mark(f"p3nmr{sbi}", "m_dve")
            wm("scalar", f"p3nmr{sbi}")
            p.act(sqfv[:, :], tfv[:, :], AF.Identity, scale=st(ST_RSTD),
                  bias=st(ST_NMR), inc="m_act")
            mark(f"p3norm{sbi}", "m_act")
            p.acopy(x1bv[:, :], sqfv[:, :], inc="m_act")
            mark(f"p3x1b{sbi}", "m_act")
            wm("sync", f"p3x1b{sbi}")
            p.dma("sync", x1f_s[sbi], sqfv, inc="d_s")
            wm("tensor", f"p3x1b{sbi}")
            for half in range(2):
                bank = 4 + half
                p.pe_bank(bank)
                for t in range(8):
                    k = half * 8 + t
                    p.tr(ps_bf(bank, t), x1bv[:, k * 128:(k + 1) * 128], idv,
                         inc="m_pe" if t == 7 else None)
                p.wait("scalar", "m_pe")
                p.acopy(x1tv[buf][:, :].rearrange("p (k d) -> p k d", d=128)
                        [:, half * 8:half * 8 + 8, :],
                        ps_bf8(bank), inc="m_act")
                p.bank_read(bank, "m_act")
            mark(f"p3x1t{sbi}", "m_act")
            wm("sync", f"p3x1t{sbi}")
            half, col = sbi // 4, (sbi % 4) * 128
            dst3 = x1t_s[half].rearrange("p (k s) -> p k s", s=512)
            src3 = x1tv[buf][:, :].rearrange("p (k d) -> p k d", d=128)
            p.dma("sync", dst3[:, 0:8, col:col + 128], src3[:, 0:8, :],
                  inc="d_s")
            p.dma("sync", dst3[:, 8:16, col:col + 128], src3[:, 8:16, :],
                  inc="d_s")
            mark(f"p3spill{sbi}", "d_s")
            if sbi + 2 < SB:
                wm("gpsimd", f"p3mm{sbi}")
                wm("gpsimd", f"p3t{sbi}")
                nx = sbi + 2
                src = ao_s[:, :, nx * 128:(nx + 1) * 128].transpose([1, 0, 2])
                p.dma("gpsimd", aotv[buf][:, 0:8, :], src[:, 0:8, :])
                p.dma("gpsimd", aotv[buf][:, 8:16, :], src[:, 8:16, :])
                p.dma("gpsimd", xresv[buf], x_in[nx * 128:(nx + 1) * 128, :])
                aol.append(p.cnt["d_g"])
        mark("p3_pe_done", "m_pe")
        mark("p3_act_done", "m_act")
        mark("p3_dve_done", "m_dve")
        mark("p3_ds_done", "d_s")
        if stage == 3:
            _finish(p, env, block)
            return nc

        A_SC, A_Y, A_H2, A_W1, A_W2 = 0, 1024, 33792, 45056, 67584
        A_X1TH, A_XR4 = 90112, 98304
        scoresv = f32v(A_SC, SBS * NEXP)
        combv = f32v(A_SC + 256, SBS * NEXP)
        t12v = f32v(A_SC + 512, 8)
        st4v = f32v(A_SC + 528, 8)
        gsv = f32v(A_SC + 560, 8)
        mskv = f32v(A_SC + 576, 16)
        m2v = f32v(A_SC + 608, 16)

        def st4(j):
            return st4v[:, j:j + 1]

        yv3 = f32v(A_Y, SBS * D).rearrange("p (t d) -> p t d", d=D)
        hv3 = bfv(A_H2, KH * STRIPE).rearrange("p (k s) -> p k s", s=STRIPE)
        w1v3 = bfv(A_W1, KD * H).rearrange("p (k h) -> p k h", h=H)
        w2v3 = bfv(A_W2, KH * D).rearrange("p (k d) -> p k d", d=D)
        x1thv = bfv(A_X1TH, KD * 512).rearrange("p (k s) -> p k s", s=512)
        xr4v = f32v(A_XR4, D)

        experts = [(sw1[0], sw2[0], None)] + [(w1[e], w2[e], e)
                                              for e in range(NEXP)]
        W1SPL = [(0, 4), (4, 4), (8, 4), (12, 4)]
        W2SPL = [(0, 3), (3, 3), (6, 3), (9, 2)]
        entries = [(t_, xi) for t_ in range(NSTRIPE)
                   for xi in range(len(experts))]

        wm("gpsimd", "p3_pe_done")
        wm("gpsimd", "p3_act_done")
        wm("gpsimd", "p3_dve_done")
        wm("gpsimd", "p3_ds_done")

        def load_w1(ei):
            wv = experts[entries[ei][1]][0].rearrange("(k p) h -> p k h", p=128)
            for (k0, kn) in W1SPL:
                p.dma("gpsimd", w1v3[:, k0:k0 + kn, :], wv[:, k0:k0 + kn, :])

        def load_w2(ei):
            wv = experts[entries[ei][1]][1].rearrange("(k p) h -> p k h", p=128)
            for (k0, kn) in W2SPL:
                p.dma("gpsimd", w2v3[:, k0:k0 + kn, :], wv[:, k0:k0 + kn, :])

        def emit_finalize(t_, sbl):
            sbg = t_ * SBS + sbl
            ysb = yv3[:, sbl, :]
            if sbg > 0:
                wm("sync", f"p4out{sbg - 1}")
            p.dma("sync", xr4v, x1f_s[sbg], inc="d_s")
            xl = p.cnt["d_s"]
            p.wait("vector", "d_s", xl)
            p.tt(ysb, ysb, xr4v, ALU.add, inc="m_dve")
            mark(f"p4add{sbg}", "m_dve")
            wm("scalar", f"p4add{sbg}")
            p.act(xr4v, ysb, AF.Identity, accum=st4(0), inc="m_act")
            p.act(xr4v, ysb, AF.Square, accum=st4(1), inc="m_act")
            p.afence()
            p.act(st4(2), st4(0), AF.Identity, scale=cc(C_INVD), inc="m_act")
            p.act(st4(3), st4(1), AF.Identity, scale=cc(C_INVD), inc="m_act")
            p.afence()
            p.act(st4(4), st4(2), AF.Square, inc="m_act")
            p.afence()
            p.act(st4(4), st4(4), AF.Identity, scale=cc(C_NEG1), inc="m_act")
            p.afence()
            p.act(st4(5), st4(3), AF.Identity, bias=st4(4), inc="m_act")
            p.afence()
            p.act(st4(6), st4(5), AF.Ln, bias=cc(C_EPS), inc="m_act")
            p.afence()
            p.act(st4(7), st4(6), AF.Exp, scale=cc(C_NEGHALF), inc="m_act")
            p.afence()
            p.act(st4(4), st4(2), AF.Identity, scale=st4(7), inc="m_act")
            p.afence()
            p.act(st4(4), st4(4), AF.Identity, scale=cc(C_NEG1), inc="m_act")
            p.afence()
            p.act(xr4v, ysb, AF.Identity, scale=st4(7), bias=st4(4),
                  inc="m_act")
            mark(f"p4norm{sbg}", "m_act")
            wm("sync", f"p4norm{sbg}")
            p.dma("sync", out_t[sbg * 128:(sbg + 1) * 128, :], xr4v,
                  inc="d_s")
            mark(f"p4out{sbg}", "d_s")

        w1_loaded, w2_loaded = {}, {}
        load_w1(0)
        w1_loaded[0] = p.cnt["d_g"]
        ia_bank = 0
        ib_bank = 0
        for ei, (t_, xi) in enumerate(entries):
            is_sh = experts[xi][2] is None
            eid = experts[xi][2]
            p.wait("tensor", "d_g", w1_loaded[ei])
            for sc in range(2):
                if not (ei == 0 and sc == 0):
                    prev = (ei, 0) if sc == 1 else (ei - 1, 1)
                    wm("gpsimd", f"Ahalf{prev[0]}_{prev[1]}")
                p.dma("gpsimd", bfv(A_X1TH, KD * 512), x1t_s[t_ * 2 + sc])
                xl = p.cnt["d_g"]
                p.wait("tensor", "d_g", xl)
                if is_sh:
                    for sbl in range(sc * 4, sc * 4 + 4):
                        p.pe_bank(4)
                        for k in range(KD):
                            p.mm(ps[4][:, 0:NEXP],
                                 x1thv[:, k, (sbl - sc * 4) * 128:
                                       (sbl - sc * 4 + 1) * 128],
                                 gateTv3[:, k, :],
                                 start=(k == 0), stop=(k == KD - 1),
                                 inc=("m_pe" if k == KD - 1 else None))
                        mark(f"gmm{t_}_{sbl}", "m_pe")
                        wm("scalar", f"gmm{t_}_{sbl}")
                        if sbl == 0 and t_ > 0:
                            wm("scalar", f"ydone{t_ - 1}")
                        p.act(scoresv[:, sbl * NEXP:(sbl + 1) * NEXP],
                              ps[4][:, 0:NEXP], AF.Sigmoid, inc="m_act")
                        p.bank_read(4, "m_act")
                        mark(f"gsc{t_}_{sbl}", "m_act")
                for hb in range(KH):
                    bank = ia_bank % 4
                    ia_bank += 1
                    p.pe_bank(bank)
                    for k in range(KD):
                        p.mm(ps[bank][:, :],
                             w1v3[:, k, hb * 128:(hb + 1) * 128],
                             x1thv[:, k, 0:512],
                             start=(k == 0), stop=(k == KD - 1),
                             inc=("m_pe" if k == KD - 1 else None))
                    mark(f"Amm{ei}_{sc}_{hb}", "m_pe")
                    wm("scalar", f"Amm{ei}_{sc}_{hb}")
                    if ei > 0 and sc == 0 and hb == 0:
                        wm("scalar", f"Bmm_done{ei - 1}")
                    p.act(hv3[:, hb, sc * 512:(sc + 1) * 512],
                          ps[bank][:, :], AF.Silu, inc="m_act")
                    p.bank_read(bank, "m_act")
                mark(f"Ahalf{ei}_{sc}", "m_pe")
            mark(f"A_done{ei}", "m_pe")
            mark(f"silu_done{ei}", "m_act")
            if ei > 0:
                wm("gpsimd", f"Bmm_done{ei - 1}")
            load_w2(ei)
            w2_loaded[ei] = p.cnt["d_g"]
            if is_sh:
                wm("vector", f"gsc{t_}_{7}")
                pairs = [(0, 1), (0, 2), (0, 3), (1, 2), (1, 3), (2, 3)]
                for sbl in range(SBS):
                    sv = scoresv[:, sbl * NEXP:(sbl + 1) * NEXP]
                    cmb = combv[:, sbl * NEXP:(sbl + 1) * NEXP]
                    for j, (a, b) in enumerate(pairs):
                        p.tt(gsv[:, 4:8], sv[:, a:16:4], sv[:, b:16:4],
                             ALU.add, inc="m_dve")
                        p.vfence()
                        if j == 0:
                            p.vcopy(gsv[:, 0:4], gsv[:, 4:8], inc="m_dve")
                        else:
                            p.tt(gsv[:, 0:4], gsv[:, 0:4], gsv[:, 4:8],
                                 ALU.max, inc="m_dve")
                        p.vfence()
                    p.tt(mskv[:, 0:2], gsv[:, 0:2], gsv[:, 2:4], ALU.max,
                         inc="m_dve")
                    p.tt(mskv[:, 2:4], gsv[:, 0:2], gsv[:, 2:4], ALU.min,
                         inc="m_dve")
                    p.vfence()
                    p.tt(mskv[:, 4:5], mskv[:, 0:1], mskv[:, 1:2], ALU.min,
                         inc="m_dve")
                    p.tt(mskv[:, 5:6], mskv[:, 2:3], mskv[:, 3:4], ALU.max,
                         inc="m_dve")
                    p.vfence()
                    p.tt(mskv[:, 6:7], mskv[:, 4:5], mskv[:, 5:6], ALU.max,
                         inc="m_dve")
                    p.vfence()
                    p.ts(gsv[:, 0:4], gsv[:, 0:4], mskv[:, 6:7], ALU.is_ge,
                         inc="m_dve")
                    p.vfence()
                    for i4 in range(4):
                        p.tt(cmb[:, i4:16:4], sv[:, i4:16:4], gsv[:, 0:4],
                             ALU.mult, inc="m_dve")
                    p.vfence()
                    p.rmax(gsv[:, 4:5], cmb, inc="m_dve")
                    p.vfence()
                    p.ts(mskv[:, 0:16], cmb, gsv[:, 4:5], ALU.is_ge,
                         inc="m_dve")
                    p.vfence()
                    p.stt(m2v[:, 0:16], mskv[:, 0:16], -1e30, cmb,
                          ALU.mult, ALU.add, inc="m_dve")
                    p.vfence()
                    p.rmax(gsv[:, 5:6], m2v[:, 0:16], inc="m_dve")
                    p.vfence()
                    p.tt(t12v[:, sbl:sbl + 1], gsv[:, 4:5], gsv[:, 5:6],
                         ALU.add, inc="m_dve")
                    p.ts(mskv[:, 0:16], cmb, gsv[:, 5:6], ALU.is_ge,
                         inc="m_dve")
                    p.vfence()
                    p.ts(t12v[:, sbl:sbl + 1], t12v[:, sbl:sbl + 1], 1e-20,
                         ALU.add, inc="m_dve")
                    p.tt(cmb, cmb, mskv[:, 0:16], ALU.mult, inc="m_dve")
                    p.vfence()
                p.recip(t12v[:, :], t12v[:, :], inc="m_dve")
                p.vfence()
                for sbl in range(SBS):
                    cmb = combv[:, sbl * NEXP:(sbl + 1) * NEXP]
                    p.ts(cmb, cmb, t12v[:, sbl:sbl + 1], ALU.mult,
                         inc="m_dve")
                p.vfence()
                mark(f"routing{t_}", "m_dve")
                if dbg4 and t_ == 0:
                    wm("sync", f"routing{t_}")
                    p.dma("sync", g_dbg[0], scoresv, inc="d_s")
                    p.dma("sync", g_dbg[1], combv, inc="d_s")
            p.wait("tensor", "d_g", w2_loaded[ei])
            wm("tensor", f"silu_done{ei}")
            for sbl in range(SBS):
                for ec in range(4):
                    bank = 4 + ib_bank % 4
                    ib_bank += 1
                    p.pe_bank(bank)
                    for k in range(KH):
                        p.mm(ps[bank][:, :],
                             hv3[:, k, sbl * 128:(sbl + 1) * 128],
                             w2v3[:, k, ec * 512:(ec + 1) * 512],
                             start=(k == 0), stop=(k == KH - 1),
                             inc=("m_pe" if k == KH - 1 else None))
                    mark(f"Bmm{ei}_{sbl}_{ec}", "m_pe")
                    wm("vector", f"Bmm{ei}_{sbl}_{ec}")
                    if not is_sh:
                        wm("vector", f"routing{t_}")
                    elif t_ > 0 and ec == 0:
                        wm("vector", f"p4norm{(t_ - 1) * SBS + sbl}")
                    ydst = yv3[:, sbl, ec * 512:(ec + 1) * 512]
                    if is_sh:
                        p.vcopy(ydst, ps[bank][:, :], inc="m_dve")
                    else:
                        p.stt(ydst, ps[bank][:, :],
                              combv[:, sbl * NEXP + eid:sbl * NEXP + eid + 1],
                              ydst, ALU.mult, ALU.add, inc="m_dve")
                    p.bank_read(bank, "m_dve")
                if xi == len(experts) - 1:
                    emit_finalize(t_, sbl)
            mark(f"Bmm_done{ei}", "m_pe")
            mark(f"yupd{ei}", "m_dve")
            if ei + 1 < len(entries):
                wm("gpsimd", f"A_done{ei}")
                load_w1(ei + 1)
                w1_loaded[ei + 1] = p.cnt["d_g"]
            if xi == len(experts) - 1:
                mark(f"ydone{t_}", "m_dve")
                if dbg4 and t_ == 0:
                    wm("sync", f"ydone{t_}")
                    p.dma("sync", y_dbg, yv3[:, 0, :], inc="d_s")

        _finish(p, env, block)
        return nc


def _finish(p, env, block):
    p.wait("sync", "d_s")
    p.wait("sync", "d_g")
    p.build(block, env)


def kernel(**inputs):
    x_full = np.ascontiguousarray(inputs["L0_attn_x"], dtype=np.float32)
    base = {
        "W_qkv": np.ascontiguousarray(inputs["L0_attn_W_qkv"], dtype=np.float32),
        "W_out": np.ascontiguousarray(inputs["L0_attn_W_out"], dtype=np.float32),
        "gate_w": np.ascontiguousarray(inputs["L0_ffn_gate_w"], dtype=np.float32),
        "w1": np.ascontiguousarray(inputs["L0_ffn_w1"], dtype=np.float32),
        "w2": np.ascontiguousarray(inputs["L0_ffn_w2"], dtype=np.float32),
        "sw1": np.ascontiguousarray(inputs["L0_ffn_sw1"], dtype=np.float32),
        "sw2": np.ascontiguousarray(inputs["L0_ffn_sw2"], dtype=np.float32),
    }
    nc = build_program(stage=4)
    in_maps = [dict(base, x=np.ascontiguousarray(x_full[b])) for b in range(8)]
    res = run_bass_kernel_spmd(nc, in_maps, core_ids=list(range(8)))
    return np.stack([np.asarray(res.results[b]["out"]) for b in range(8)],
                    axis=0)


# version 07
# speedup vs baseline: 1.6088x; 1.6088x over previous
import numpy as np
import ml_dtypes
from contextlib import ExitStack

import concourse.bass as bass
import concourse.mybir as mybir
from concourse.bass_utils import run_bass_kernel_spmd
from concourse.bass_core_rust import IndirectOffsetOnAxis

F32 = mybir.dt.float32
UI32 = mybir.dt.uint32
BF16 = mybir.dt.bfloat16
AF = mybir.ActivationFunctionType
ALU = mybir.AluOpType
AX = mybir.AxisListType

S, D, NH, DH, CH = 4096, 2048, 16, 128, 256
NCH = S // CH
H = 1408
NEXP = 16
KD, KH = D // 128, H // 128
SB = S // 128
STRIPE = 1024
NSTRIPE = S // STRIPE
SBS = STRIPE // 128
EPS = 1e-5
ARENA = 103000
PAIRS = [(0, 1), (0, 2), (0, 3), (1, 2), (1, 3), (2, 3)]

C_EPS, C_NEGHALF, C_INVD, C_ONE, C_EPS20, C_NEG1 = 0, 1, 2, 3, 4, 5
NCONST = 8


def _mk_consts():
    v = np.zeros((128, NCONST), np.float32)
    v[:, C_EPS] = EPS
    v[:, C_NEGHALF] = -0.5
    v[:, C_INVD] = 1.0 / D
    v[:, C_ONE] = 1.0
    v[:, C_EPS20] = 1e-20
    v[:, C_NEG1] = -1.0
    return v


def _rope_tables():
    inv = 1.0 / (10000.0 ** (np.arange(0, DH, 2, dtype=np.float32) / DH))
    pos = np.arange(S, dtype=np.float32)
    emb = np.concatenate([np.outer(pos, inv), np.outer(pos, inv)], axis=1)
    cos = np.cos(emb).T.astype(np.float32)
    sin = np.sin(emb).T.astype(np.float32)
    sgn = np.where(np.arange(DH) % 2 == 0, -1.0, 1.0).astype(np.float32)[:, None]
    return (np.ascontiguousarray(cos.astype(ml_dtypes.bfloat16)),
            np.ascontiguousarray((sin * sgn).astype(ml_dtypes.bfloat16)))


class Prog:
    ENGS = ("sync", "gpsimd", "scalar", "vector", "tensor")

    def __init__(self):
        self.ops = {e: [] for e in self.ENGS}
        self.cnt = {}
        self.bank_guard = [None] * 8

    def sem(self, name):
        self.cnt[name] = 0

    def bump(self, s, k):
        self.cnt[s] += k
        return self.cnt[s]

    def wait(self, eng, sem, val=None):
        v = self.cnt[sem] if val is None else val
        if v <= 0:
            return
        self.ops[eng].append(lambda h, env, s=sem, vv=v: h.wait_ge(env[s], vv))

    def op(self, eng, f, inc=None, inck=1):
        if inc is not None:
            self.bump(inc, inck)

            def g(h, env, f=f, s=inc, k=inck):
                f(h, env).then_inc(env[s], k)
            self.ops[eng].append(g)
        else:
            self.ops[eng].append(lambda h, env, f=f: f(h, env))

    def dma(self, eng, out, in_, inc="d_g"):
        self.op(eng, lambda h, env, o=out, i=in_: h.dma_start(out=o, in_=i),
                inc=inc, inck=16)

    def mm(self, out, lhsT, rhs, start, stop, inc=None):
        self.op("tensor",
                lambda h, env, o=out, l=lhsT, r=rhs, a=start, b=stop:
                h.matmul(o, l, r, start=a, stop=b), inc=inc)

    def tr(self, out, in_, ident, inc=None):
        self.op("tensor",
                lambda h, env, o=out, i=in_, d=ident: h.transpose(o, i, d),
                inc=inc)

    def acopy(self, out, in_, inc=None):
        self.op("scalar", lambda h, env, o=out, i=in_: h.copy(o, i), inc=inc)

    def act(self, out, in_, func, bias=0.0, scale=1.0, accum=None, inc=None):
        self.op("scalar",
                lambda h, env, o=out, i=in_, f=func, b=bias, s=scale, a=accum:
                h.activation(o, i, f, bias=b, scale=s, accum_out=a), inc=inc)

    def vcopy(self, out, in_, inc=None):
        self.op("vector", lambda h, env, o=out, i=in_: h.tensor_copy(o, i),
                inc=inc)

    def tt(self, out, a, b, alu, inc=None):
        self.op("vector",
                lambda h, env, o=out, x=a, y=b, u=alu:
                h.tensor_tensor(o, x, y, u), inc=inc)

    def ts(self, out, a, s1, op0, s2=None, op1=None, inc=None):
        op1 = ALU.bypass if op1 is None else op1
        self.op("vector",
                lambda h, env, o=out, x=a, u=s1, v=s2, p0=op0, p1=op1:
                h.tensor_scalar(o, x, u, v, p0, p1), inc=inc)

    def stt(self, out, a, scal, b, op0, op1, inc=None):
        self.op("vector",
                lambda h, env, o=out, x=a, s=scal, y=b, p0=op0, p1=op1:
                h.scalar_tensor_tensor(o, x, s, y, p0, p1), inc=inc)

    def rsum(self, out, in_, inc=None):
        self.op("vector",
                lambda h, env, o=out, i=in_: h.reduce_sum(o, i, axis=AX.X),
                inc=inc)

    def rmax(self, out, in_, inc=None):
        self.op("vector",
                lambda h, env, o=out, i=in_: h.reduce_max(o, i, axis=AX.X),
                inc=inc)

    def recip(self, out, in_, inc=None):
        self.op("vector", lambda h, env, o=out, i=in_: h.reciprocal(o, i),
                inc=inc)

    def memset(self, eng, ap_, val, inc=None):
        self.op(eng, lambda h, env, a=ap_, v=val: h.memset(a, v), inc=inc)

    def vfence(self):
        self.wait("vector", "m_dve")

    def afence(self):
        self.wait("scalar", "m_act")

    def pe_bank(self, b):
        g = self.bank_guard[b]
        if g is not None:
            self.wait("tensor", g[0], g[1])
            self.bank_guard[b] = None

    def bank_read(self, b, sem):
        self.bank_guard[b] = (sem, self.cnt[sem])

    def build(self, block, env):
        def mk(ops):
            def body(h):
                for f in ops:
                    f(h, env)
            return body
        block.sync(mk(self.ops["sync"]))
        block.gpsimd(mk(self.ops["gpsimd"]))
        block.scalar(mk(self.ops["scalar"]))
        block.vector(mk(self.ops["vector"]))
        block.tensor(mk(self.ops["tensor"]))


def build_program(stage=4, dbg4=False, dbgtab=False):
    nc = bass.Bass("TRN2", target_bir_lowering=False, debug=False)

    x_in = nc.dram_tensor("x", [S, D], F32, kind="ExternalInput").ap()
    Wqkv = nc.dram_tensor("W_qkv", [3 * D, D], F32, kind="ExternalInput").ap()
    Wout = nc.dram_tensor("W_out", [D, D], F32, kind="ExternalInput").ap()
    gw = nc.dram_tensor("gate_w", [NEXP, D], F32, kind="ExternalInput").ap()
    w1 = nc.dram_tensor("w1", [NEXP, D, H], F32, kind="ExternalInput").ap()
    w2 = nc.dram_tensor("w2", [NEXP, H, D], F32, kind="ExternalInput").ap()
    sw1 = nc.dram_tensor("sw1", [1, D, H], F32, kind="ExternalInput").ap()
    sw2 = nc.dram_tensor("sw2", [1, H, D], F32, kind="ExternalInput").ap()

    def scr(name, shape, dtype, dbg):
        kind = "ExternalOutput" if stage == dbg else "Internal"
        return nc.dram_tensor(name, shape, dtype, kind=kind).ap()

    qd_s = scr("qd_s", [NH, 128, S], BF16, 1)
    kd_s = scr("kd_s", [NH, 128, S], BF16, 1)
    vd_s = scr("vd_s", [NH, 128, S], BF16, 1)
    ao_s = scr("ao_s", [NH, 128, S], BF16, 2)
    x1f_s = scr("x1f_s", [SB, 128, D], F32, 3)
    x1t_s = scr("x1t_s", [2 * NSTRIPE, 128, KD * 512], BF16, 3)
    x1r_s = scr("x1r_s", [S, D], BF16, None)
    y_s = nc.dram_tensor("y_s", [S + 128, D], F32, kind="Internal").ap()
    idxs_s = nc.dram_tensor("idxs_s", [3 * NEXP * 128 * 8 + 256, 1],
                            mybir.dt.uint32,
                            kind="ExternalOutput" if dbgtab else "Internal").ap()
    out_t = (nc.dram_tensor("out", [S, D], F32, kind="ExternalOutput").ap()
             if stage >= 4 else None)
    den_dbg = (nc.dram_tensor("den_dbg", [NH, 128, SB], F32,
                              kind="ExternalOutput").ap() if stage == 2 else None)
    kv_dbg = (nc.dram_tensor("kv_dbg", [2, 128, SB * 129], BF16,
                             kind="ExternalOutput").ap() if stage == 2 else None)
    ss_dbg = (nc.dram_tensor("ss_dbg", [128, 512 + 256 + 129], F32,
                             kind="ExternalOutput").ap() if stage == 2 else None)
    g_dbg = (nc.dram_tensor("g_dbg", [3, 128, 128], F32,
                            kind="ExternalOutput").ap() if dbg4 else None)
    rt_dbg = (nc.dram_tensor("rt_dbg", [128, 1280], F32,
                             kind="ExternalOutput").ap() if dbgtab else None)
    rtu_dbg = (nc.dram_tensor("rtu_dbg", [128, 192], mybir.dt.uint32,
                              kind="ExternalOutput").ap() if dbgtab else None)
    y_dbg = (nc.dram_tensor("y_dbg", [128, D], F32,
                            kind="ExternalOutput").ap() if dbg4 else None)

    cos_np, sins_np = _rope_tables()
    maskT_np = (np.arange(384)[None, :] >= (np.arange(128)[:, None] + 128)
                ).astype(np.float32)
    pswap_np = np.zeros((128, 128), dtype=ml_dtypes.bfloat16)
    for i in range(0, 128, 2):
        pswap_np[i, i + 1] = 1.0
        pswap_np[i + 1, i] = 1.0
    c_cos = nc.inline_tensor(cos_np, "c_cos").ap()
    c_sins = nc.inline_tensor(sins_np, "c_sins").ap()
    c_mask = nc.inline_tensor(maskT_np, "c_mask").ap()
    c_psw = nc.inline_tensor(pswap_np, "c_psw").ap()
    c_id = nc.inline_tensor(np.eye(128, dtype=ml_dtypes.bfloat16), "c_id").ap()
    c_cc = nc.inline_tensor(_mk_consts(), "c_cc").ap()
    c_idf = nc.inline_tensor(np.eye(16, dtype=np.float32), "c_idf").ap()
    c_iotaE = nc.inline_tensor(
        np.tile(np.arange(NEXP, dtype=np.float32)[None, :], (128, 1)),
        "c_iotaE").ap()
    ids_np = (np.arange(SB)[None, :] * 128 +
              np.arange(128)[:, None]).astype(np.uint32)
    c_ids = nc.inline_tensor(ids_np, "c_ids").ap()
    padinit_np = (S + np.arange(128, dtype=np.uint32)[:, None] +
                  np.zeros((1, 128), np.uint32))
    c_padi = nc.inline_tensor(padinit_np, "c_padi").ap()
    dump_np = (2.0 * NEXP * 128 * 8 +
               np.arange(128, dtype=np.float32)[:, None])
    c_dump = nc.inline_tensor(dump_np, "c_dump").ap()
    shids_np = np.zeros((128, 32), np.uint32)
    for b_ in range(4):
        for jj in range(8):
            shids_np[:, b_ * 8 + jj] = b_ * 1024 + jj * 128 + np.arange(128)
    c_shids = nc.inline_tensor(shids_np, "c_shids").ap()

    ctx = ExitStack()
    with ctx:
        arena = ctx.enter_context(nc.sbuf_tensor("arena", [128, ARENA], BF16))

        def bfv(off, n):
            return arena[:, off:off + n]

        def f32v(off, n):
            return arena[:, off:off + 2 * n].bitcast(F32)

        ps = [ctx.enter_context(nc.psum_tensor(f"ps{i}", [128, 512], F32))
              for i in range(8)]

        def ps_bf(b, t):
            return ps[b][:, t * 64:(t + 1) * 64].bitcast(BF16)

        def ps_bf8(b):
            return ps[b][:, 0:512].bitcast(BF16).rearrange(
                "p (t d) -> p t d", d=128)

        p = Prog()
        env = {}
        for sname in ("d_g", "d_s", "m_pe", "m_act", "m_dve", "m_pool"):
            p.sem(sname)
            env[sname] = ctx.enter_context(nc.semaphore(sname))
        block = ctx.enter_context(nc.Block())

        M = {}

        def mark(name, sem):
            M[name] = (sem, p.cnt[sem])

        def wm(eng, name):
            s_, v_ = M[name]
            p.wait(eng, s_, v_)

        A_COS, A_SINS, A_MASK, A_PSW, A_ID = 0, 4096, 8192, 8960, 9088
        A_GATET, A_CCOL = 102400, 102800
        cosv, sinsv = bfv(A_COS, S), bfv(A_SINS, S)
        maskv = f32v(A_MASK, 384)
        pswv, idv = bfv(A_PSW, 128), bfv(A_ID, 128)
        gateTv3 = bfv(A_GATET, KD * NEXP).rearrange("p (k e) -> p k e", e=NEXP)
        ccol = f32v(A_CCOL, NCONST)

        def cc(j):
            return ccol[:, j:j + 1]

        p.dma("gpsimd", cosv, c_cos)
        p.dma("gpsimd", sinsv, c_sins)
        p.dma("gpsimd", maskv, c_mask)
        p.dma("gpsimd", pswv, c_psw)
        p.dma("gpsimd", idv, c_id)
        p.dma("gpsimd", ccol, c_cc)
        A_GWB = 9600
        gwb = bfv(A_GWB, D)
        p.dma("gpsimd", gwb[0:NEXP, :], gw)
        mark("p0_loads", "d_g")
        wm("tensor", "p0_loads")
        for half in range(2):
            bank = 6 + half
            p.pe_bank(bank)
            for t in range(8):
                k = half * 8 + t
                p.tr(ps_bf(bank, t)[:, 0:NEXP],
                     gwb[0:NEXP, k * 128:(k + 1) * 128],
                     idv[0:NEXP, 0:NEXP], inc="m_pe" if t == 7 else None)
            p.wait("scalar", "m_pe")
            p.acopy(gateTv3[:, half * 8:half * 8 + 8, :],
                    ps_bf8(bank)[:, :, 0:NEXP], inc="m_act")
            p.bank_read(bank, "m_act")
        mark("gateT", "m_act")

        A_XT, A_WB, A_WT, A_XB, A_QPL = 9600, 75136, 79232, 83328, 87424
        A_R1, A_R2, A_R3 = 95616, 97664, 99712
        xTv = [bfv(A_XT + k * S, S) for k in range(KD)]
        xT8 = [bfv(A_XT + h8 * 8 * S, 8 * S).rearrange("p (t s) -> p t s", s=S)
               for h8 in range(2)]
        wbv = [bfv(A_WB + i * D, D) for i in range(2)]
        wTv = [bfv(A_WT + i * D, D) for i in range(2)]
        xbv = [bfv(A_XB + i * D, D) for i in range(2)]
        qplv = [bfv(A_QPL + i * S, S) for i in range(2)]
        r1p = [f32v(A_R1 + i * 1024, 512) for i in range(2)]
        r2p = [f32v(A_R2 + i * 1024, 512) for i in range(2)]
        r3p = [f32v(A_R3 + i * 1024, 512) for i in range(2)]

        for sbi in range(SB):
            if sbi >= 2:
                wm("gpsimd", f"p1a_tr{sbi - 2}")
            p.dma("gpsimd", xbv[sbi % 2], x_in[sbi * 128:(sbi + 1) * 128, :])
            xl = p.cnt["d_g"]
            p.wait("tensor", "d_g", xl)
            for half in range(2):
                bank = 6 + half
                p.pe_bank(bank)
                for t in range(8):
                    k = half * 8 + t
                    p.tr(ps_bf(bank, t),
                         xbv[sbi % 2][:, k * 128:(k + 1) * 128], idv,
                         inc="m_pe" if t == 7 else None)
                if half == 1:
                    mark(f"p1a_tr{sbi}", "m_pe")
                p.wait("scalar", "m_pe")
                p.acopy(xT8[half][:, :, sbi * 128:sbi * 128 + 128],
                        ps_bf8(bank), inc="m_act")
                p.bank_read(bank, "m_act")
        mark("p1a_done", "m_act")

        GROUPS = [(0, 1, 2), (3, 4, 5), (6, 7)]
        GBANKS = [(0, 1, 2), (3, 4, 5), (0, 1)]
        wload = []
        p.dma("gpsimd", wbv[0], Wqkv[0:128, :])
        wload.append(p.cnt["d_g"])
        p.dma("gpsimd", wbv[1], Wqkv[128:256, :])
        wload.append(p.cnt["d_g"])
        wm("tensor", "p1a_done")

        for jb in range(48):
            which, head = jb // NH, jb % NH
            buf = jb % 2
            p.wait("tensor", "d_g", wload[jb])
            for half in range(2):
                bank = 6 + half
                p.pe_bank(bank)
                for t in range(8):
                    k = half * 8 + t
                    p.tr(ps_bf(bank, t),
                         wbv[buf][:, k * 128:(k + 1) * 128], idv,
                         inc="m_pe" if t == 7 else None)
                if half == 1:
                    mark(f"wtr{jb}", "m_pe")
                p.wait("scalar", "m_pe")
                p.acopy(bfv(A_WT + buf * D + half * 1024, 1024).rearrange(
                    "p (t d) -> p t d", d=128), ps_bf8(bank), inc="m_act")
                p.bank_read(bank, "m_act")
            mark(f"wtrev{jb}", "m_act")
            if jb + 2 < 48:
                wm("gpsimd", f"wtr{jb}")
                p.dma("gpsimd", wbv[buf], Wqkv[(jb + 2) * 128:(jb + 3) * 128, :])
                wload.append(p.cnt["d_g"])
            if jb >= 2:
                wm("scalar", f"spill{jb - 2}")
            wm("tensor", f"wtrev{jb}")
            for gi, grp in enumerate(GROUPS):
                banks = GBANKS[gi]
                for b in banks:
                    p.pe_bank(b)
                for k in range(KD):
                    lhsT = wTv[buf][:, k * 128:(k + 1) * 128]
                    for ci, c in enumerate(grp):
                        p.mm(ps[banks[ci]][:, :], lhsT,
                             xTv[k][:, c * 512:(c + 1) * 512],
                             start=(k == 0), stop=(k == KD - 1),
                             inc=("m_pe" if (k == KD - 1 and
                                             ci == len(grp) - 1) else None))
                mark(f"qkvmm{jb}_{gi}", "m_pe")
                wm("scalar", f"qkvmm{jb}_{gi}")
                for ci, c in enumerate(grp):
                    p.acopy(qplv[buf][:, c * 512:(c + 1) * 512],
                            ps[banks[ci]][:, :], inc="m_act")
                    p.bank_read(banks[ci], "m_act")
                    mark(f"qkvev{jb}_{c}", "m_act")
            if which < 2:
                for c in range(8):
                    par = c % 2
                    bank = 6 + par
                    cs = slice(c * 512, (c + 1) * 512)
                    p.pe_bank(bank)
                    wm("tensor", f"qkvev{jb}_{c}")
                    p.mm(ps[bank][:, :], pswv, qplv[buf][:, cs],
                         start=True, stop=True, inc="m_pe")
                    mark(f"psw{jb}_{c}", "m_pe")
                    if c >= 2:
                        wm("vector", f"rope2_{jb}_{c - 2}")
                    wm("vector", f"psw{jb}_{c}")
                    p.tt(r2p[par], ps[bank][:, :], sinsv[:, cs], ALU.mult,
                         inc="m_dve")
                    p.bank_read(bank, "m_dve")
                    p.tt(r1p[par], qplv[buf][:, cs], cosv[:, cs], ALU.mult)
                    p.tt(r1p[par], r1p[par], r2p[par], ALU.add, inc="m_dve")
                    mark(f"rope1_{jb}_{c}", "m_dve")
                    wm("scalar", f"rope1_{jb}_{c}")
                    p.act(r2p[par], r1p[par], AF.Exp)
                    p.act(r3p[par], r1p[par], AF.Relu, inc="m_act")
                    mark(f"rope2_{jb}_{c}", "m_act")
                    wm("vector", f"rope2_{jb}_{c}")
                    p.ts(r2p[par], r2p[par], 1.0, ALU.min)
                    p.tt(r1p[par], r2p[par], r3p[par], ALU.add)
                    p.vcopy(qplv[buf][:, cs], r1p[par], inc="m_dve")
                    mark(f"ropewb{jb}_{c}", "m_dve")
                wm("sync", f"ropewb{jb}_{7}")
                p.dma("sync", (qd_s if which == 0 else kd_s)[head],
                      qplv[buf], inc="d_s")
            else:
                wm("sync", f"qkvev{jb}_{7}")
                p.dma("sync", vd_s[head], qplv[buf], inc="d_s")
            mark(f"spill{jb}", "d_s")
        mark("p1_pe_done", "m_pe")
        mark("p1_act_done", "m_act")
        mark("p1_dve_done", "m_dve")
        if stage == 1:
            _finish(p, env, block)
            return nc

        A_QH, A_KH, A_VH = 9600, 17792, 25984
        A_KS, A_VE = 34176, 38272
        A_ATT, A_SSB = 42400, 43424
        A_NUMF, A_DEN, A_DENR = 43568, 51760, 51824
        A_AOB, A_AOPL = 51888, 52912
        A_WOT, A_WOB = 61104, 93872
        qhv = [bfv(A_QH + i * S, S) for i in range(2)]
        khv = [bfv(A_KH + i * S, S) for i in range(2)]
        vhv = [bfv(A_VH + i * S, S) for i in range(2)]
        ksv3 = bfv(A_KS, SB * 128).rearrange("p (t d) -> p t d", d=128)
        vev_flat = bfv(A_VE, SB * 129)
        vev3 = vev_flat.rearrange("p (t d) -> p t d", d=129)
        attv2 = [bfv(A_ATT, 512), bfv(A_ATT + 512, 512)]
        ssbv = bfv(A_SSB, 129)
        numfv = f32v(A_NUMF, SB * 128)
        denv, denrv = f32v(A_DEN, SB), f32v(A_DENR, SB)
        aobv = bfv(A_AOB, 1024)
        aoplv = [bfv(A_AOPL + i * S, S) for i in range(2)]
        woTv = [bfv(A_WOT + k * D, D) for k in range(KD)]
        wobv = [bfv(A_WOB + i * D, D) for i in range(2)]

        wm("gpsimd", "p1_pe_done")
        wm("gpsimd", "p1_act_done")
        wm("gpsimd", "p1_dve_done")
        wm("gpsimd", f"spill{47}")
        p.memset("gpsimd", vev_flat, 1.0, inc="m_pool")
        mark("ve_ones", "m_pool")

        hload = []
        for hh in range(2):
            for src, dstb in ((qd_s, qhv), (kd_s, khv), (vd_s, vhv)):
                p.dma("gpsimd", dstb[hh], src[hh])
            hload.append(p.cnt["d_g"])

        for h_i in range(NH):
            buf = h_i % 2
            p.wait("tensor", "d_g", hload[h_i])
            for src, kind in ((khv[buf], "ks"), (vhv[buf], "ve")):
                for batch in range(4):
                    bank = 6 + (batch % 2)
                    p.pe_bank(bank)
                    for t in range(8):
                        tt_ = batch * 8 + t
                        p.tr(ps_bf(bank, t),
                             src[:, tt_ * 128:(tt_ + 1) * 128], idv,
                             inc="m_pe" if t == 7 else None)
                    p.wait("scalar", "m_pe")
                    if kind == "ve":
                        wm("scalar", "ve_ones")
                        dest = vev3[:, batch * 8:batch * 8 + 8, 0:128]
                    else:
                        dest = ksv3[:, batch * 8:batch * 8 + 8, :]
                    p.acopy(dest, ps_bf8(bank), inc="m_act")
                    p.bank_read(bank, "m_act")
            mark(f"kv_tr{h_i}", "m_act")
            if stage == 2 and h_i == 0:
                wm("sync", f"kv_tr{h_i}")
                p.dma("sync", kv_dbg[0][:, 0:SB * 128],
                      bfv(A_KS, SB * 128), inc="d_s")
                p.dma("sync", kv_dbg[1], vev_flat, inc="d_s")
                mark("kvdbg", "d_s")
            if h_i > 0:
                wm("vector", f"aocast{h_i - 1}")
                wm("tensor", f"amask{h_i - 1}_{NCH - 1}")
                wm("tensor", f"numev{h_i - 1}_{NCH - 1}")
                wm("tensor", f"scopy{h_i - 1}_{NCH - 1}")
            wm("tensor", f"kv_tr{h_i}")
            for b in (1, 2, 3):
                p.pe_bank(b)

            def att_group(c, h_i=h_i, buf=buf):
                bank = 0 if c % 2 == 0 else 4
                s0 = c * CH
                p.pe_bank(bank)
                p.mm(ps[bank][:, 0:256], khv[buf][:, s0:s0 + 128],
                     qhv[buf][:, s0:s0 + 256], start=True, stop=True)
                p.mm(ps[bank][:, 256:512], khv[buf][:, s0 + 128:s0 + 256],
                     qhv[buf][:, s0:s0 + 256], start=True, stop=True,
                     inc="m_pe")
                mark(f"att{h_i}_{c}", "m_pe")
                wm("vector", f"att{h_i}_{c}")
                if c >= 2:
                    wm("vector", f"num{h_i}_{c - 2}")
                av = attv2[c % 2]
                p.tt(av[:, 0:256], ps[bank][:, 0:256], maskv[:, 128:384],
                     ALU.mult)
                p.tt(av[:, 256:512], ps[bank][:, 256:512], maskv[:, 0:256],
                     ALU.mult, inc="m_dve")
                p.bank_read(bank, "m_dve")
                mark(f"amask{h_i}_{c}", "m_dve")

            def num_group(c, h_i=h_i, buf=buf):
                s0 = c * CH
                av = attv2[c % 2]
                wm("tensor", f"amask{h_i}_{c}")
                if c > 0:
                    wm("tensor", f"scopy{h_i}_{c}")
                    wm("tensor", f"numev{h_i}_{c - 1}")
                for lh in range(2):
                    psn = ps[1 + lh]
                    lc = lh * 128
                    p.mm(psn[:, 0:129], av[:, lc:lc + 128],
                         vev3[:, 2 * c, :], start=True, stop=False)
                    if c == 0:
                        p.mm(psn[:, 0:129], av[:, 256 + lc:256 + lc + 128],
                             vev3[:, 2 * c + 1, :], start=False, stop=True,
                             inc=("m_pe" if lh == 1 else None))
                    else:
                        p.mm(psn[:, 0:129], av[:, 256 + lc:256 + lc + 128],
                             vev3[:, 2 * c + 1, :], start=False, stop=False)
                        p.mm(psn[:, 0:129], qhv[buf][:, s0 + lc:s0 + lc + 128],
                             ssbv[:, 0:129], start=False, stop=True,
                             inc=("m_pe" if lh == 1 else None))
                mark(f"num{h_i}_{c}", "m_pe")
                p.mm(ps[3][:, 0:129], ksv3[:, 2 * c, :], vev3[:, 2 * c, :],
                     start=(c == 0), stop=False)
                p.mm(ps[3][:, 0:129], ksv3[:, 2 * c + 1, :],
                     vev3[:, 2 * c + 1, :], start=False, stop=(c == NCH - 1),
                     inc="m_pe")
                mark(f"upd{h_i}_{c}", "m_pe")
                if c < NCH - 1:
                    wm("scalar", f"upd{h_i}_{c}")
                    p.acopy(ssbv[:, 0:129], ps[3][:, 0:129], inc="m_act")
                    mark(f"scopy{h_i}_{c + 1}", "m_act")
                wm("vector", f"num{h_i}_{c}")
                for lh in range(2):
                    i_ = 2 * c + lh
                    p.vcopy(numfv[:, i_ * 128:(i_ + 1) * 128],
                            ps[1 + lh][:, 0:128])
                    p.vcopy(denv[:, i_:i_ + 1], ps[1 + lh][:, 128:129],
                            inc="m_dve")
                mark(f"numev{h_i}_{c}", "m_dve")

            att_group(0)
            for c in range(NCH):
                if c + 1 < NCH:
                    att_group(c + 1)
                num_group(c)
            p.vfence()
            p.ts(denv[:, :], denv[:, :], 1e-6, ALU.max, inc="m_dve")
            p.vfence()
            p.recip(denrv[:, :], denv[:, :], inc="m_dve")
            p.vfence()
            for i_ in range(SB):
                p.ts(numfv[:, i_ * 128:(i_ + 1) * 128],
                     numfv[:, i_ * 128:(i_ + 1) * 128],
                     denrv[:, i_:i_ + 1], ALU.mult,
                     inc="m_dve" if i_ == SB - 1 else None)
            mark(f"aoscale{h_i}", "m_dve")
            if stage == 2 and den_dbg is not None:
                wm("sync", f"aoscale{h_i}")
                p.dma("sync", den_dbg[h_i], denv, inc="d_s")
            if h_i >= 2:
                wm("scalar", f"aospill{h_i - 2}")
            wm("scalar", f"aoscale{h_i}")
            for batch in range(4):
                bank = 6 + (batch % 2)
                p.acopy(aobv[:, :],
                        numfv[:, batch * 1024:(batch + 1) * 1024],
                        inc="m_act")
                mark(f"aocastb{h_i}_{batch}", "m_act")
                p.pe_bank(bank)
                wm("tensor", f"aocastb{h_i}_{batch}")
                for t in range(8):
                    p.tr(ps_bf(bank, t), aobv[:, t * 128:(t + 1) * 128], idv,
                         inc="m_pe" if t == 7 else None)
                p.wait("scalar", "m_pe")
                p.acopy(aoplv[buf][:, :].rearrange("p (t s) -> p t s", s=128)
                        [:, batch * 8:batch * 8 + 8, :],
                        ps_bf8(bank), inc="m_act")
                p.bank_read(bank, "m_act")
            mark(f"aocast{h_i}", "m_act")
            wm("sync", f"aocast{h_i}")
            p.dma("sync", ao_s[h_i], aoplv[buf], inc="d_s")
            mark(f"aospill{h_i}", "d_s")
            if h_i + 2 < NH:
                wm("gpsimd", f"num{h_i}_{NCH - 1}")
                for src, dstb in ((qd_s, qhv), (kd_s, khv), (vd_s, vhv)):
                    p.dma("gpsimd", dstb[buf], src[h_i + 2])
                hload.append(p.cnt["d_g"])
            if h_i >= 2:
                wm("gpsimd", f"wotr{h_i - 2}")
            p.dma("gpsimd", wobv[buf], Wout[h_i * 128:(h_i + 1) * 128, :])
            wol = p.cnt["d_g"]
            p.wait("tensor", "d_g", wol)
            for half in range(2):
                bank = 6 + half
                p.pe_bank(bank)
                for t in range(8):
                    k = half * 8 + t
                    p.tr(ps_bf(bank, t),
                         wobv[buf][:, k * 128:(k + 1) * 128], idv,
                         inc="m_pe" if t == 7 else None)
                if half == 1:
                    mark(f"wotr{h_i}", "m_pe")
                p.wait("scalar", "m_pe")
                p.acopy(bfv(A_WOT + half * 8 * D, 8 * D).rearrange(
                    "p (t s) -> p t s", s=D)
                    [:, :, h_i * 128:h_i * 128 + 128],
                    ps_bf8(bank), inc="m_act")
                p.bank_read(bank, "m_act")
        mark("p2_pe_done", "m_pe")
        mark("p2_act_done", "m_act")
        mark("p2_dve_done", "m_dve")
        if stage == 2:
            _finish(p, env, block)
            return nc

        A_AOT, A_XRES, A_TF = 9600, 13696, 21888
        A_SQF, A_X1B, A_X1T, A_ST = 25984, 30080, 32128, 36224
        aotv = [bfv(A_AOT + i * D, D).rearrange("p (k d) -> p k d", d=128)
                for i in range(2)]
        xresv = [f32v(A_XRES + i * 2 * D, D) for i in range(2)]
        tfv = f32v(A_TF, D)
        sqfv = f32v(A_SQF, D)
        x1bv = bfv(A_X1B, D)
        x1tv = [bfv(A_X1T + i * D, D) for i in range(2)]
        stv = f32v(A_ST, 16)
        ST_SUM, ST_SQ, ST_MEAN, ST_Q1, ST_M2, ST_VAR, ST_RSTD, ST_NMR = range(8)

        def st(j):
            return stv[:, j:j + 1]

        def u32v(off, n):
            return arena[:, off:off + 2 * n].bitcast(UI32)

        scoresv = f32v(0, 512)
        combv = f32v(1024, 512)
        m1mv = f32v(2048, 512)
        m2xv = f32v(3072, 512)
        rkTv = f32v(4096, 512)
        c1v, c2v = f32v(5120, 32), f32v(5184, 32)
        c1uv, c2uv = u32v(5120, 32), u32v(5184, 32)
        e1v, e2v = f32v(5248, 32), f32v(5312, 32)
        rk1v, rk2v = f32v(5376, 32), f32v(5440, 32)
        offf1, offf2 = f32v(5504, 32), f32v(5568, 32)
        offu1, offu2 = u32v(5632, 32), u32v(5696, 32)
        offc1, offc2 = u32v(7040, 32), u32v(7104, 32)
        tmpav, tmpbv = f32v(5760, 32), f32v(5824, 32)
        gsv = f32v(5888, 8)
        mskv = f32v(5904, 16)
        m2v = f32v(5936, 16)
        t12v = f32v(5968, 16)
        onesc8 = f32v(6000, 8)
        c_iotaEv = f32v(6016, 16)
        mskbfv = bfv(6048, 512)
        idcolv = u32v(6560, 32)
        idfv = arena[0:16, 6624:6656].bitcast(F32)
        shidtv = u32v(6656, 32)
        idxtv = [u32v(6720, 8), u32v(6736, 8)]
        cvtv = [u32v(6752, 8), u32v(6768, 8)]
        cvtfv = [f32v(6752, 8), f32v(6768, 8)]
        hugev = u32v(6784, 128)
        gidxtv = [u32v(7168, 8), u32v(7184, 8)]
        gtmpv = f32v(7200, 8)
        dumpc = f32v(7232, 1)

        wm("gpsimd", "p2_pe_done")
        wm("gpsimd", "p2_act_done")
        wm("gpsimd", "p2_dve_done")
        wm("gpsimd", f"aospill{NH - 1}")
        p.dma("gpsimd", c_iotaEv, c_iotaE)
        p.dma("gpsimd", idfv, c_idf)
        p.dma("gpsimd", idcolv, c_ids)
        p.dma("gpsimd", shidtv, c_shids)
        p.dma("gpsimd", dumpc, c_dump[:, 0:1])
        mark("routec", "d_g")
        wm("vector", "routec")
        p.memset("vector", onesc8, 1.0, inc="m_dve")
        wm("tensor", "routec")
        aol = []
        for i in range(2):
            src = ao_s[:, :, i * 128:(i + 1) * 128].transpose([1, 0, 2])
            p.dma("gpsimd", aotv[i][:, 0:8, :], src[:, 0:8, :])
            p.dma("gpsimd", aotv[i][:, 8:16, :], src[:, 8:16, :])
            p.dma("gpsimd", xresv[i], x_in[i * 128:(i + 1) * 128, :])
            aol.append(p.cnt["d_g"])
        for sbi in range(SB):
            buf = sbi % 2
            p.wait("tensor", "d_g", aol[sbi])
            for b in range(4):
                p.pe_bank(b)
            for k in range(KD):
                lhsT = aotv[buf][:, k, :]
                for ec in range(4):
                    p.mm(ps[ec][:, :], lhsT,
                         woTv[k][:, ec * 512:(ec + 1) * 512],
                         start=(k == 0), stop=(k == KD - 1),
                         inc=("m_pe" if (k == KD - 1 and ec == 3) else None))
            mark(f"p3mm{sbi}", "m_pe")
            if sbi > 0:
                wm("vector", f"p3norm{sbi - 1}")
            wm("vector", f"p3mm{sbi}")
            for ec in range(4):
                p.tt(tfv[:, ec * 512:(ec + 1) * 512], ps[ec][:, :],
                     xresv[buf][:, ec * 512:(ec + 1) * 512], ALU.add,
                     inc="m_dve")
                p.bank_read(ec, "m_dve")
            p.rsum(st(ST_SUM), tfv[:, :], inc="m_dve")
            mark(f"p3t{sbi}", "m_dve")
            p.vfence()
            if sbi > 0:
                wm("scalar", f"p3spill{sbi - 1}")
            wm("scalar", f"p3t{sbi}")
            p.act(sqfv[:, :], tfv[:, :], AF.Square, accum=st(ST_SQ),
                  inc="m_act")
            mark(f"p3sq{sbi}", "m_act")
            wm("vector", f"p3sq{sbi}")
            p.ts(st(ST_MEAN), st(ST_SUM), cc(C_INVD), ALU.mult, inc="m_dve")
            p.ts(st(ST_Q1), st(ST_SQ), cc(C_INVD), ALU.mult, inc="m_dve")
            p.vfence()
            p.tt(st(ST_M2), st(ST_MEAN), st(ST_MEAN), ALU.mult, inc="m_dve")
            p.vfence()
            p.tt(st(ST_VAR), st(ST_Q1), st(ST_M2), ALU.subtract, inc="m_dve")
            mark(f"p3var{sbi}", "m_dve")
            wm("scalar", f"p3var{sbi}")
            p.act(st(ST_Q1), st(ST_VAR), AF.Ln, bias=cc(C_EPS), inc="m_act")
            p.afence()
            p.act(st(ST_RSTD), st(ST_Q1), AF.Exp, scale=cc(C_NEGHALF),
                  inc="m_act")
            mark(f"p3rstd{sbi}", "m_act")
            wm("vector", f"p3rstd{sbi}")
            p.tt(st(ST_NMR), st(ST_MEAN), st(ST_RSTD), ALU.mult, inc="m_dve")
            p.vfence()
            p.ts(st(ST_NMR), st(ST_NMR), -1.0, ALU.mult, inc="m_dve")
            mark(f"p3nmr{sbi}", "m_dve")
            wm("scalar", f"p3nmr{sbi}")
            p.act(sqfv[:, :], tfv[:, :], AF.Identity, scale=st(ST_RSTD),
                  bias=st(ST_NMR), inc="m_act")
            mark(f"p3norm{sbi}", "m_act")
            p.acopy(x1bv[:, :], sqfv[:, :], inc="m_act")
            mark(f"p3x1b{sbi}", "m_act")
            wm("sync", f"p3x1b{sbi}")
            p.dma("sync", x1f_s[sbi], sqfv, inc="d_s")
            wm("tensor", f"p3x1b{sbi}")
            for half in range(2):
                bank = 4 + half
                p.pe_bank(bank)
                for t in range(8):
                    k = half * 8 + t
                    p.tr(ps_bf(bank, t), x1bv[:, k * 128:(k + 1) * 128], idv,
                         inc="m_pe" if t == 7 else None)
                p.wait("scalar", "m_pe")
                p.acopy(x1tv[buf][:, :].rearrange("p (k d) -> p k d", d=128)
                        [:, half * 8:half * 8 + 8, :],
                        ps_bf8(bank), inc="m_act")
                p.bank_read(bank, "m_act")
            mark(f"p3x1t{sbi}", "m_act")
            p.dma("sync", x1r_s[sbi * 128:(sbi + 1) * 128, :], x1bv,
                  inc="d_s")
            mark(f"p3spill{sbi}", "d_s")
            wm("tensor", f"p3x1t{sbi}")
            p.pe_bank(6)
            for k in range(KD):
                p.mm(ps[6][:, 0:NEXP],
                     x1tv[buf][:, k * 128:(k + 1) * 128],
                     gateTv3[:, k, :],
                     start=(k == 0), stop=(k == KD - 1),
                     inc=("m_pe" if k == KD - 1 else None))
            mark(f"gmm{sbi}", "m_pe")
            wm("scalar", f"gmm{sbi}")
            p.act(scoresv[:, sbi * NEXP:(sbi + 1) * NEXP], ps[6][:, 0:NEXP],
                  AF.Sigmoid, inc="m_act")
            p.bank_read(6, "m_act")
            mark(f"gsc{sbi}", "m_act")
            wm("vector", f"gsc{sbi}")
            sv = scoresv[:, sbi * NEXP:(sbi + 1) * NEXP]
            cmb = combv[:, sbi * NEXP:(sbi + 1) * NEXP]
            m1c = m1mv[:, sbi * NEXP:(sbi + 1) * NEXP]
            m2c = m2xv[:, sbi * NEXP:(sbi + 1) * NEXP]
            for j, (a, b) in enumerate(PAIRS):
                p.tt(gsv[:, 4:8], sv[:, a:16:4], sv[:, b:16:4], ALU.add,
                     inc="m_dve")
                p.vfence()
                if j == 0:
                    p.vcopy(gsv[:, 0:4], gsv[:, 4:8], inc="m_dve")
                else:
                    p.tt(gsv[:, 0:4], gsv[:, 0:4], gsv[:, 4:8], ALU.max,
                         inc="m_dve")
                p.vfence()
            p.tt(mskv[:, 0:2], gsv[:, 0:2], gsv[:, 2:4], ALU.max, inc="m_dve")
            p.tt(mskv[:, 2:4], gsv[:, 0:2], gsv[:, 2:4], ALU.min, inc="m_dve")
            p.vfence()
            p.tt(mskv[:, 4:5], mskv[:, 0:1], mskv[:, 1:2], ALU.min,
                 inc="m_dve")
            p.tt(mskv[:, 5:6], mskv[:, 2:3], mskv[:, 3:4], ALU.max,
                 inc="m_dve")
            p.vfence()
            p.tt(mskv[:, 6:7], mskv[:, 4:5], mskv[:, 5:6], ALU.max,
                 inc="m_dve")
            p.vfence()
            p.ts(gsv[:, 0:4], gsv[:, 0:4], mskv[:, 6:7], ALU.is_ge,
                 inc="m_dve")
            p.vfence()
            for i4 in range(4):
                p.tt(cmb[:, i4:16:4], sv[:, i4:16:4], gsv[:, 0:4], ALU.mult,
                     inc="m_dve")
            p.vfence()
            p.rmax(mskv[:, 8:9], cmb, inc="m_dve")
            p.vfence()
            p.ts(m1c, cmb, mskv[:, 8:9], ALU.is_ge, inc="m_dve")
            p.vfence()
            p.stt(m2v[:, 0:16], m1c, -1e30, cmb, ALU.mult, ALU.add,
                  inc="m_dve")
            p.vfence()
            p.rmax(mskv[:, 9:10], m2v[:, 0:16], inc="m_dve")
            p.vfence()
            p.tt(t12v[:, 0:1], mskv[:, 8:9], mskv[:, 9:10], ALU.add,
                 inc="m_dve")
            p.ts(m2c, cmb, mskv[:, 9:10], ALU.is_ge, inc="m_dve")
            p.vfence()
            p.ts(t12v[:, 0:1], t12v[:, 0:1], 1e-20, ALU.add, inc="m_dve")
            p.tt(m2c, m2c, m1c, ALU.subtract, inc="m_dve")
            p.vfence()
            p.tt(mskv[:, 0:16], m1c, m2c, ALU.add, inc="m_dve")
            p.vfence()
            p.tt(cmb, cmb, mskv[:, 0:16], ALU.mult, inc="m_dve")
            p.ts(t12v[:, 0:8], onesc8[:, :], t12v[:, 0:1], ALU.mult,
                 inc="m_dve")
            p.vfence()
            p.recip(t12v[:, 8:16], t12v[:, 0:8], inc="m_dve")
            p.vfence()
            p.ts(cmb, cmb, t12v[:, 8:9], ALU.mult, inc="m_dve")
            p.vfence()
            p.tt(m2v[:, 0:16], cmb, m1c, ALU.mult, inc="m_dve")
            p.vfence()
            p.rsum(c1v[:, sbi:sbi + 1], m2v[:, 0:16], inc="m_dve")
            p.tt(m2v[:, 0:16], cmb, m2c, ALU.mult, inc="m_dve")
            p.vfence()
            p.rsum(c2v[:, sbi:sbi + 1], m2v[:, 0:16], inc="m_dve")
            p.tt(m2v[:, 0:16], c_iotaEv[:, :], m1c, ALU.mult, inc="m_dve")
            p.vfence()
            p.rsum(e1v[:, sbi:sbi + 1], m2v[:, 0:16], inc="m_dve")
            p.tt(m2v[:, 0:16], c_iotaEv[:, :], m2c, ALU.mult, inc="m_dve")
            p.vfence()
            p.rsum(e2v[:, sbi:sbi + 1], m2v[:, 0:16], inc="m_dve")
            p.vfence()
            mark(f"p3route{sbi}", "m_dve")
            if sbi + 2 < SB:
                wm("gpsimd", f"p3mm{sbi}")
                wm("gpsimd", f"p3t{sbi}")
                nx = sbi + 2
                src = ao_s[:, :, nx * 128:(nx + 1) * 128].transpose([1, 0, 2])
                p.dma("gpsimd", aotv[buf][:, 0:8, :], src[:, 0:8, :])
                p.dma("gpsimd", aotv[buf][:, 8:16, :], src[:, 8:16, :])
                p.dma("gpsimd", xresv[buf], x_in[nx * 128:(nx + 1) * 128, :])
                aol.append(p.cnt["d_g"])
        mark("p3_pe_done", "m_pe")
        mark("p3_act_done", "m_act")
        mark("p3_dve_done", "m_dve")
        mark("p3_ds_done", "d_s")
        if stage == 3:
            _finish(p, env, block)
            return nc

        CAP, CAPJ = 1024, 8
        A_XG, A_XET, A_H4 = 9600, 17792, 34176
        A_W14, A_W24, A_YEV = 45440, 67968, 90496
        A_MSKT, A_CUM, A_ONES = 17792, 25984, 34176
        xgv = [bfv(A_XG + i * D, D) for i in range(2)]
        xet3 = bfv(A_XET, KD * CAP).rearrange("p (k s) -> p k s", s=CAP)
        h4v = bfv(A_H4, KH * CAP).rearrange("p (k s) -> p k s", s=CAP)
        w14 = bfv(A_W14, KD * H).rearrange("p (k h) -> p k h", h=H)
        w24 = bfv(A_W24, KH * D).rearrange("p (k d) -> p k d", d=D)
        yevv = [f32v(A_YEV + i * 2 * D, D) for i in range(2)]
        finy = [f32v(A_XG + i * 2 * D, D) for i in range(2)]
        finx = [f32v(A_XET + i * 2 * D, D) for i in range(2)]
        fst = [f32v(A_H4 + i * 32, 8) for i in range(2)]
        mskT = arena[0:16, A_MSKT:A_MSKT + 8192].bitcast(F32)
        cumT = arena[0:16, A_CUM:A_CUM + 8192].bitcast(F32)
        onesT = arena[0:16, A_ONES:A_ONES + 8192].bitcast(F32)
        _flat = idxs_s.squeeze(1)
        idxs3 = _flat[0:NEXP * 128 * CAPJ].rearrange(
            "(e p j) -> e p j", p=128, j=CAPJ)
        cvals3 = _flat[NEXP * 128 * CAPJ:2 * NEXP * 128 * CAPJ].rearrange(
            "(e p j) -> e p j", p=128, j=CAPJ)
        idxs2 = idxs_s
        yrows = y_s

        wm("gpsimd", "p3_pe_done")
        wm("gpsimd", "p3_act_done")
        wm("gpsimd", "p3_dve_done")
        wm("gpsimd", "p3_ds_done")
        wm("vector", "p3_ds_done")

        wm("vector", f"p3route{SB - 1}")
        p.ts(mskbfv[:, :], combv[:, :], 0.0, ALU.is_gt, inc="m_dve")
        mark("mskbf", "m_dve")
        for sb_i in range(SB):
            bank = 6 + (sb_i % 2)
            p.pe_bank(bank)
            wm("tensor", "mskbf")
            p.tr(ps[bank][:, 0:64].bitcast(BF16)[0:16, :],
                 mskbfv[:, sb_i * NEXP:(sb_i + 1) * NEXP], idv, inc="m_pe")
            mark(f"mtr{sb_i}", "m_pe")
            wm("scalar", f"mtr{sb_i}")
            p.acopy(mskT[:, sb_i * 128:(sb_i + 1) * 128],
                    ps[bank][:, 0:64].bitcast(BF16)[0:16, :], inc="m_act")
            p.bank_read(bank, "m_act")
        mark("mskT_done", "m_act")
        p.memset("vector", onesT, 1.0, inc="m_dve")
        wm("vector", "mskT_done")
        p.op("vector",
             lambda h, env: h.tensor_tensor_scan(
                 cumT, onesT, mskT, 0.0, ALU.mult, ALU.add), inc="m_dve")
        p.vfence()
        p.tt(cumT, cumT, mskT, ALU.subtract, inc="m_dve")
        mark("rank_done", "m_dve")
        for sb_i in range(SB):
            bank = 6 + (sb_i % 2)
            p.pe_bank(bank)
            wm("tensor", "rank_done")
            p.tr(ps[bank][:, 0:16], cumT[:, sb_i * 128:(sb_i + 1) * 128],
                 idfv[:, :], inc="m_pe")
            mark(f"rtr{sb_i}", "m_pe")
            wm("scalar", f"rtr{sb_i}")
            p.acopy(rkTv[:, sb_i * NEXP:(sb_i + 1) * NEXP],
                    ps[bank][:, 0:16], inc="m_act")
            p.bank_read(bank, "m_act")
            mark(f"rkT{sb_i}", "m_act")
            wm("vector", f"rkT{sb_i}")
            sl = slice(sb_i * NEXP, (sb_i + 1) * NEXP)
            p.tt(m2v[:, 0:16], rkTv[:, sl], m1mv[:, sl], ALU.mult,
                 inc="m_dve")
            p.vfence()
            p.rsum(rk1v[:, sb_i:sb_i + 1], m2v[:, 0:16], inc="m_dve")
            p.tt(m2v[:, 0:16], rkTv[:, sl], m2xv[:, sl], ALU.mult,
                 inc="m_dve")
            p.vfence()
            p.rsum(rk2v[:, sb_i:sb_i + 1], m2v[:, 0:16], inc="m_dve")
            p.vfence()
        for rkv_, ev_, offf_, offu_, offc_ in (
                (rk1v, e1v, offf1, offu1, offc1),
                (rk2v, e2v, offf2, offu2, offc2)):
            p.ts(tmpav, rkv_, 1.0 / 128, ALU.mult, s2=-0.49609375,
                 op1=ALU.add, inc="m_dve")
            p.vfence()
            p.vcopy(offu_, tmpav, inc="m_dve")
            p.vfence()
            p.vcopy(tmpav, offu_, inc="m_dve")
            p.vfence()
            p.stt(tmpbv, tmpav, -128.0, rkv_, ALU.mult, ALU.add,
                  inc="m_dve")
            p.vfence()
            p.ts(tmpbv, tmpbv, float(CAPJ), ALU.mult, inc="m_dve")
            p.vfence()
            p.tt(tmpbv, tmpbv, tmpav, ALU.add, inc="m_dve")
            p.vfence()
            p.stt(tmpav, ev_, float(CAP), tmpbv, ALU.mult, ALU.add,
                  inc="m_dve")
            p.ts(tmpbv, rkv_, float(CAP), ALU.is_ge, inc="m_dve")
            p.vfence()
            p.ts(offf_, tmpbv, -1.0, ALU.mult, s2=1.0, op1=ALU.add,
                 inc="m_dve")
            p.vfence()
            p.tt(tmpav, tmpav, offf_, ALU.mult, inc="m_dve")
            p.vfence()
            p.stt(tmpav, tmpbv, dumpc[:, 0:1], tmpav, ALU.mult, ALU.add,
                  inc="m_dve")
            p.vfence()
            p.vcopy(offu_, tmpav, inc="m_dve")
            p.ts(tmpav, tmpav, float(NEXP * CAP), ALU.add, inc="m_dve")
            p.vfence()
            p.vcopy(offc_, tmpav, inc="m_dve")
            p.vfence()
        mark("offs_done", "m_dve")
        p.memset("gpsimd", hugev, 0, inc="m_pool")
        p.op("gpsimd", lambda h, env: h.dma_start(
            out=_flat[NEXP * 128 * CAPJ:2 * NEXP * 128 * CAPJ].rearrange(
                "(p j) -> p j", p=128), in_=hugev), inc="d_g", inck=16)
        p.wait("gpsimd", "d_g", p.cnt["d_g"])
        p.dma("gpsimd", hugev, c_padi)
        p.wait("gpsimd", "d_g", p.cnt["d_g"])
        p.op("gpsimd", lambda h, env: h.dma_start(
            out=_flat[0:NEXP * 128 * CAPJ].rearrange("(p j) -> p j", p=128),
            in_=hugev), inc="d_g", inck=16)
        init_v = p.cnt["d_g"]
        p.wait("gpsimd", "d_g", init_v)
        wm("gpsimd", "offs_done")
        for sb_i in range(SB):
            for offu_, offc_, cv_ in ((offu1, offc1, c1uv),
                                      (offu2, offc2, c2uv)):
                p.wait("gpsimd", "d_g", p.cnt["d_g"])
                p.op("gpsimd",
                     lambda h, env, o=offu_[:, sb_i:sb_i + 1],
                     i=idcolv[:, sb_i:sb_i + 1]:
                     h.indirect_dma_start(
                         idxs2, IndirectOffsetOnAxis(o, 0), i, None),
                     inc="d_g", inck=16)
                p.op("gpsimd",
                     lambda h, env, o=offc_[:, sb_i:sb_i + 1],
                     i=cv_[:, sb_i:sb_i + 1]:
                     h.indirect_dma_start(
                         idxs2, IndirectOffsetOnAxis(o, 0), i, None),
                     inc="d_g", inck=16)
        mark("idx_done", "d_g")
        if dbgtab:
            wm("sync", "offs_done")
            p.dma("sync", rt_dbg[:, 0:512], scoresv, inc="d_s")
            p.dma("sync", rt_dbg[:, 512:1024], combv, inc="d_s")
            p.dma("sync", rt_dbg[:, 1024:1056], offf1, inc="d_s")
            p.dma("sync", rt_dbg[:, 1056:1088], offf2, inc="d_s")
            p.dma("sync", rt_dbg[:, 1088:1120], rk1v, inc="d_s")
            p.dma("sync", rt_dbg[:, 1120:1152], rk2v, inc="d_s")
            p.dma("sync", rt_dbg[:, 1152:1184], e1v, inc="d_s")
            p.dma("sync", rt_dbg[:, 1184:1216], e2v, inc="d_s")
            p.dma("sync", rt_dbg[:, 1216:1248], c1v, inc="d_s")
            p.dma("sync", rt_dbg[:, 1248:1280], c2v, inc="d_s")
            p.dma("sync", rtu_dbg[:, 0:32], offu1, inc="d_s")
            p.dma("sync", rtu_dbg[:, 32:64], offu2, inc="d_s")
            p.dma("sync", rtu_dbg[:, 64:96], offc1, inc="d_s")
            p.dma("sync", rtu_dbg[:, 96:128], offc2, inc="d_s")
            p.dma("sync", rtu_dbg[:, 128:160], idcolv, inc="d_s")
            p.dma("sync", rtu_dbg[:, 160:192], shidtv, inc="d_s")
            _finish(p, env, block)
            return nc

        entries4 = [("sh", b) for b in range(4)] + \
                   [("ex", e) for e in range(NEXP)]

        def load_w(which_w, src):
            wv = src.rearrange("(k p) h -> p k h", p=128)
            spl = W1SPL if which_w == 1 else W2SPL
            dst = w14 if which_w == 1 else w24
            for (k0, kn) in spl:
                p.dma("gpsimd", dst[:, k0:k0 + kn, :], wv[:, k0:k0 + kn, :])

        W1SPL = [(0, 4), (4, 4), (8, 4), (12, 4)]
        W2SPL = [(0, 3), (3, 3), (6, 3), (9, 2)]
        load_w(1, sw1[0])
        load_w(2, sw2[0])
        w1_ld = {0: p.cnt["d_g"]}
        w2_ld = {0: p.cnt["d_g"]}
        ia_b = 0
        ib_b = 0
        last_scatter = {}
        wm("gpsimd", f"p3spill{SB - 1}")
        for ei, (kind, en) in enumerate(entries4):
            is_sh = (kind == "sh")
            if not is_sh:
                wm("gpsimd", "idx_done")
                if ei >= 2:
                    wm("gpsimd", f"bev{ei - 2}_{CAPJ - 1}")
                p.dma("gpsimd", idxtv[ei % 2], idxs3[en])
                p.dma("gpsimd", cvtv[ei % 2], cvals3[en])
            idx_v = p.cnt["d_g"]
            if not is_sh:
                p.wait("vector", "d_g", idx_v)
                if ei >= 2:
                    wm("vector", f"xet_done{ei - 2}")
                p.vcopy(gtmpv, idxtv[ei % 2], inc="m_dve")
                p.vfence()
                p.ts(gtmpv, gtmpv, float(S - 1), ALU.min, inc="m_dve")
                p.vfence()
                p.vcopy(gidxtv[ei % 2], gtmpv, inc="m_dve")
                mark(f"gidx{ei}", "m_dve")
            for jj in range(CAPJ):
                if ei > 0:
                    wm("gpsimd", f"A_done{ei - 1}")
                if jj >= 2:
                    wm("gpsimd", f"xtr{ei}_{jj - 2}")
                if is_sh:
                    icol = shidtv[:, en * 8 + jj:en * 8 + jj + 1]
                    p.wait("gpsimd", "d_g", idx_v)
                else:
                    icol = gidxtv[ei % 2][:, jj:jj + 1]
                    wm("gpsimd", f"gidx{ei}")
                p.op("gpsimd",
                     lambda h, env, o=xgv[jj % 2], ic=icol:
                     h.indirect_dma_start(
                         o, None, x1r_s, IndirectOffsetOnAxis(ic, 0)),
                     inc="d_g", inck=16)
                gv = p.cnt["d_g"]
                p.wait("tensor", "d_g", gv)
                for half in range(2):
                    bank = half
                    p.pe_bank(bank)
                    for t in range(8):
                        k = half * 8 + t
                        p.tr(ps_bf(bank, t),
                             xgv[jj % 2][:, k * 128:(k + 1) * 128], idv,
                             inc="m_pe" if t == 7 else None)
                    if half == 1:
                        mark(f"xtr{ei}_{jj}", "m_pe")
                    p.wait("scalar", "m_pe")
                    p.acopy(xet3[:, half * 8:half * 8 + 8,
                                 jj * 128:jj * 128 + 128],
                            ps_bf8(bank), inc="m_act")
                    p.bank_read(bank, "m_act")
            mark(f"xet_done{ei}", "m_act")
            p.wait("tensor", "d_g", w1_ld[0 if is_sh else ei])
            wm("tensor", f"xet_done{ei}")
            for sc2 in range(2):
                for hb in range(KH):
                    bank = ia_b % 4
                    ia_b += 1
                    p.pe_bank(bank)
                    for k in range(KD):
                        p.mm(ps[bank][:, :], w14[:, k, hb * 128:(hb + 1) * 128],
                             xet3[:, k, sc2 * 512:(sc2 + 1) * 512],
                             start=(k == 0), stop=(k == KD - 1),
                             inc=("m_pe" if k == KD - 1 else None))
                    mark(f"Amm{ei}_{sc2}_{hb}", "m_pe")
                    wm("scalar", f"Amm{ei}_{sc2}_{hb}")
                    if ei > 0 and sc2 == 0 and hb == 0:
                        wm("scalar", f"Bmm_done{ei - 1}")
                    p.act(h4v[:, hb, sc2 * 512:(sc2 + 1) * 512],
                          ps[bank][:, :], AF.Silu, inc="m_act")
                    p.bank_read(bank, "m_act")
            mark(f"A_done{ei}", "m_pe")
            mark(f"silu_done{ei}", "m_act")
            mark(f"gath_done{ei}", "m_pe")
            if not is_sh and ei + 0 >= 4:
                pass
            if ei >= 4 or (is_sh and en == 3):
                pass
            p.wait("tensor", "d_g", w2_ld[0 if is_sh else ei])
            wm("tensor", f"silu_done{ei}")
            for sbl in range(CAPJ):
                for ec in range(4):
                    bank = 4 + ib_b % 4
                    ib_b += 1
                    p.pe_bank(bank)
                    for k in range(KH):
                        p.mm(ps[bank][:, :],
                             h4v[:, k, sbl * 128:(sbl + 1) * 128],
                             w24[:, k, ec * 512:(ec + 1) * 512],
                             start=(k == 0), stop=(k == KH - 1),
                             inc=("m_pe" if k == KH - 1 else None))
                    mark(f"Bmm{ei}_{sbl}_{ec}", "m_pe")
                    wm("vector", f"Bmm{ei}_{sbl}_{ec}")
                    if sbl >= 2 and ec == 0:
                        wm("vector", f"scat{ei}_{sbl - 2}")
                    if sbl < 2 and ec == 0 and ei > 0:
                        wm("vector", f"scat{ei - 1}_{sbl + 6}")
                    ydst = yevv[sbl % 2][:, ec * 512:(ec + 1) * 512]
                    if is_sh:
                        p.vcopy(ydst, ps[bank][:, :],
                                inc="m_dve")
                    else:
                        p.ts(ydst, ps[bank][:, :],
                             cvtfv[ei % 2][:, sbl:sbl + 1], ALU.mult,
                             inc="m_dve")
                    p.bank_read(bank, "m_dve")
                mark(f"bev{ei}_{sbl}", "m_dve")
                wm("gpsimd", f"bev{ei}_{sbl}")
                if ei in last_scatter:
                    p.wait("gpsimd", "d_g", last_scatter[ei])
                elif ei > 0 and (ei - 1) in last_scatter:
                    pass
                if ei == 4 and sbl == 0:
                    wm("gpsimd", "shared_scat_done")
                if ei > 4 and sbl == 0:
                    wm("gpsimd", f"scat_fin{ei - 1}")
                if is_sh:
                    icol2 = shidtv[:, en * 8 + sbl:en * 8 + sbl + 1]
                else:
                    icol2 = idxtv[ei % 2][:, sbl:sbl + 1]
                cop = ALU.bypass if is_sh else ALU.add
                p.op("gpsimd",
                     lambda h, env, ic=icol2, src=yevv[sbl % 2], co=cop:
                     h.indirect_dma_start(
                         yrows, IndirectOffsetOnAxis(ic, 0), src, None,
                         compute_op=co),
                     inc="d_g", inck=16)
                mark(f"scat{ei}_{sbl}", "d_g")
            mark(f"Bmm_done{ei}", "m_pe")
            mark(f"scat_fin{ei}", "d_g")
            if is_sh and en == 3:
                mark("shared_scat_done", "d_g")
            nx = ei + 1
            if nx < len(entries4):
                nkind, nen = entries4[nx]
                if nkind == "ex":
                    wm("gpsimd", f"A_done{ei}")
                    load_w(1, w1[nen])
                    w1_ld[nx] = p.cnt["d_g"]
                    wm("gpsimd", f"Bmm_done{ei}")
                    load_w(2, w2[nen])
                    w2_ld[nx] = p.cnt["d_g"]
                else:
                    w1_ld[nx] = w1_ld[0]
                    w2_ld[nx] = w2_ld[0]

        all_scat = p.cnt["d_g"]
        wm("sync", f"scat_fin{len(entries4) - 1}")
        for sbi in range(SB):
            buf = sbi % 2
            if sbi >= 2:
                wm("gpsimd", f"p4out{sbi - 2}")
            p.wait("gpsimd", "d_g", all_scat)
            p.dma("gpsimd", finy[buf], yrows[sbi * 128:(sbi + 1) * 128, :])
            yl = p.cnt["d_g"]
            p.dma("sync", finx[buf], x1f_s[sbi], inc="d_s")
            xl = p.cnt["d_s"]
            p.wait("vector", "d_g", yl)
            p.wait("vector", "d_s", xl)
            ysb = finy[buf]
            p.tt(ysb, ysb, finx[buf], ALU.add, inc="m_dve")
            mark(f"p4add{sbi}", "m_dve")
            wm("scalar", f"p4add{sbi}")
            st4 = fst[buf]
            p.act(finx[buf], ysb, AF.Identity, accum=st4[:, 0:1], inc="m_act")
            p.act(finx[buf], ysb, AF.Square, accum=st4[:, 1:2], inc="m_act")
            p.afence()
            p.act(st4[:, 2:3], st4[:, 0:1], AF.Identity, scale=cc(C_INVD),
                  inc="m_act")
            p.act(st4[:, 3:4], st4[:, 1:2], AF.Identity, scale=cc(C_INVD),
                  inc="m_act")
            p.afence()
            p.act(st4[:, 4:5], st4[:, 2:3], AF.Square, inc="m_act")
            p.afence()
            p.act(st4[:, 4:5], st4[:, 4:5], AF.Identity, scale=cc(C_NEG1),
                  inc="m_act")
            p.afence()
            p.act(st4[:, 5:6], st4[:, 3:4], AF.Identity, bias=st4[:, 4:5],
                  inc="m_act")
            p.afence()
            p.act(st4[:, 6:7], st4[:, 5:6], AF.Ln, bias=cc(C_EPS),
                  inc="m_act")
            p.afence()
            p.act(st4[:, 7:8], st4[:, 6:7], AF.Exp, scale=cc(C_NEGHALF),
                  inc="m_act")
            p.afence()
            p.act(st4[:, 4:5], st4[:, 2:3], AF.Identity, scale=st4[:, 7:8],
                  inc="m_act")
            p.afence()
            p.act(st4[:, 4:5], st4[:, 4:5], AF.Identity, scale=cc(C_NEG1),
                  inc="m_act")
            p.afence()
            p.act(finx[buf], ysb, AF.Identity, scale=st4[:, 7:8],
                  bias=st4[:, 4:5], inc="m_act")
            mark(f"p4norm{sbi}", "m_act")
            wm("sync", f"p4norm{sbi}")
            p.dma("sync", out_t[sbi * 128:(sbi + 1) * 128, :], finx[buf],
                  inc="d_s")
            mark(f"p4out{sbi}", "d_s")


        _finish(p, env, block)
        return nc


def _finish(p, env, block):
    p.wait("sync", "d_s")
    p.wait("sync", "d_g")
    p.build(block, env)


def kernel(**inputs):
    x_full = np.ascontiguousarray(inputs["L0_attn_x"], dtype=np.float32)
    base = {
        "W_qkv": np.ascontiguousarray(inputs["L0_attn_W_qkv"], dtype=np.float32),
        "W_out": np.ascontiguousarray(inputs["L0_attn_W_out"], dtype=np.float32),
        "gate_w": np.ascontiguousarray(inputs["L0_ffn_gate_w"], dtype=np.float32),
        "w1": np.ascontiguousarray(inputs["L0_ffn_w1"], dtype=np.float32),
        "w2": np.ascontiguousarray(inputs["L0_ffn_w2"], dtype=np.float32),
        "sw1": np.ascontiguousarray(inputs["L0_ffn_sw1"], dtype=np.float32),
        "sw2": np.ascontiguousarray(inputs["L0_ffn_sw2"], dtype=np.float32),
    }
    nc = build_program(stage=4)
    in_maps = [dict(base, x=np.ascontiguousarray(x_full[b])) for b in range(8)]
    res = run_bass_kernel_spmd(nc, in_maps, core_ids=list(range(8)))
    return np.stack([np.asarray(res.results[b]["out"]) for b in range(8)],
                    axis=0)


# version 09 (final)
# speedup vs baseline: 1.7369x; 1.0796x over previous
import numpy as np
import ml_dtypes
from contextlib import ExitStack

import concourse.bass as bass
import concourse.mybir as mybir
from concourse.bass_utils import run_bass_kernel_spmd
from concourse.bass_core_rust import IndirectOffsetOnAxis

F32 = mybir.dt.float32
UI32 = mybir.dt.uint32
BF16 = mybir.dt.bfloat16
AF = mybir.ActivationFunctionType
ALU = mybir.AluOpType
AX = mybir.AxisListType

S, D, NH, DH, CH = 4096, 2048, 16, 128, 256
NCH = S // CH
H = 1408
NEXP = 16
KD, KH = D // 128, H // 128
SB = S // 128
STRIPE = 1024
NSTRIPE = S // STRIPE
SBS = STRIPE // 128
EPS = 1e-5
ARENA = 103000
PAIRS = [(0, 1), (0, 2), (0, 3), (1, 2), (1, 3), (2, 3)]

C_EPS, C_NEGHALF, C_INVD, C_ONE, C_EPS20, C_NEG1 = 0, 1, 2, 3, 4, 5
NCONST = 8


def _mk_consts():
    v = np.zeros((128, NCONST), np.float32)
    v[:, C_EPS] = EPS
    v[:, C_NEGHALF] = -0.5
    v[:, C_INVD] = 1.0 / D
    v[:, C_ONE] = 1.0
    v[:, C_EPS20] = 1e-20
    v[:, C_NEG1] = -1.0
    return v


def _rope_tables():
    inv = 1.0 / (10000.0 ** (np.arange(0, DH, 2, dtype=np.float32) / DH))
    pos = np.arange(S, dtype=np.float32)
    emb = np.concatenate([np.outer(pos, inv), np.outer(pos, inv)], axis=1)
    cos = np.cos(emb).T.astype(np.float32)
    sin = np.sin(emb).T.astype(np.float32)
    sgn = np.where(np.arange(DH) % 2 == 0, -1.0, 1.0).astype(np.float32)[:, None]
    return (np.ascontiguousarray(cos.astype(ml_dtypes.bfloat16)),
            np.ascontiguousarray((sin * sgn).astype(ml_dtypes.bfloat16)))


class Prog:
    ENGS = ("sync", "gpsimd", "scalar", "vector", "tensor")

    def __init__(self):
        self.ops = {e: [] for e in self.ENGS}
        self.cnt = {}
        self.bank_guard = [None] * 8

    def sem(self, name):
        self.cnt[name] = 0

    def bump(self, s, k):
        self.cnt[s] += k
        return self.cnt[s]

    def wait(self, eng, sem, val=None):
        v = self.cnt[sem] if val is None else val
        if v <= 0:
            return
        self.ops[eng].append(lambda h, env, s=sem, vv=v: h.wait_ge(env[s], vv))

    def op(self, eng, f, inc=None, inck=1):
        if inc is not None:
            self.bump(inc, inck)

            def g(h, env, f=f, s=inc, k=inck):
                f(h, env).then_inc(env[s], k)
            self.ops[eng].append(g)
        else:
            self.ops[eng].append(lambda h, env, f=f: f(h, env))

    def dma(self, eng, out, in_, inc="d_g"):
        self.op(eng, lambda h, env, o=out, i=in_: h.dma_start(out=o, in_=i),
                inc=inc, inck=16)

    def mm(self, out, lhsT, rhs, start, stop, inc=None):
        self.op("tensor",
                lambda h, env, o=out, l=lhsT, r=rhs, a=start, b=stop:
                h.matmul(o, l, r, start=a, stop=b), inc=inc)

    def tr(self, out, in_, ident, inc=None):
        self.op("tensor",
                lambda h, env, o=out, i=in_, d=ident: h.transpose(o, i, d),
                inc=inc)

    def acopy(self, out, in_, inc=None):
        self.op("scalar", lambda h, env, o=out, i=in_: h.copy(o, i), inc=inc)

    def act(self, out, in_, func, bias=0.0, scale=1.0, accum=None, inc=None):
        self.op("scalar",
                lambda h, env, o=out, i=in_, f=func, b=bias, s=scale, a=accum:
                h.activation(o, i, f, bias=b, scale=s, accum_out=a), inc=inc)

    def vcopy(self, out, in_, inc=None):
        self.op("vector", lambda h, env, o=out, i=in_: h.tensor_copy(o, i),
                inc=inc)

    def tt(self, out, a, b, alu, inc=None):
        self.op("vector",
                lambda h, env, o=out, x=a, y=b, u=alu:
                h.tensor_tensor(o, x, y, u), inc=inc)

    def ts(self, out, a, s1, op0, s2=None, op1=None, inc=None):
        op1 = ALU.bypass if op1 is None else op1
        self.op("vector",
                lambda h, env, o=out, x=a, u=s1, v=s2, p0=op0, p1=op1:
                h.tensor_scalar(o, x, u, v, p0, p1), inc=inc)

    def stt(self, out, a, scal, b, op0, op1, inc=None):
        self.op("vector",
                lambda h, env, o=out, x=a, s=scal, y=b, p0=op0, p1=op1:
                h.scalar_tensor_tensor(o, x, s, y, p0, p1), inc=inc)

    def rsum(self, out, in_, inc=None):
        self.op("vector",
                lambda h, env, o=out, i=in_: h.reduce_sum(o, i, axis=AX.X),
                inc=inc)

    def rmax(self, out, in_, inc=None):
        self.op("vector",
                lambda h, env, o=out, i=in_: h.reduce_max(o, i, axis=AX.X),
                inc=inc)

    def recip(self, out, in_, inc=None):
        self.op("vector", lambda h, env, o=out, i=in_: h.reciprocal(o, i),
                inc=inc)

    def memset(self, eng, ap_, val, inc=None):
        self.op(eng, lambda h, env, a=ap_, v=val: h.memset(a, v), inc=inc)

    def vfence(self):
        self.wait("vector", "m_dve")

    def afence(self):
        self.wait("scalar", "m_act")

    def pe_bank(self, b):
        g = self.bank_guard[b]
        if g is not None:
            self.wait("tensor", g[0], g[1])
            self.bank_guard[b] = None

    def bank_read(self, b, sem):
        self.bank_guard[b] = (sem, self.cnt[sem])

    def build(self, block, env):
        def mk(ops):
            def body(h):
                for f in ops:
                    f(h, env)
            return body
        block.sync(mk(self.ops["sync"]))
        block.gpsimd(mk(self.ops["gpsimd"]))
        block.scalar(mk(self.ops["scalar"]))
        block.vector(mk(self.ops["vector"]))
        block.tensor(mk(self.ops["tensor"]))


def build_program(stage=4, dbg4=False, dbgtab=False):
    nc = bass.Bass("TRN2", target_bir_lowering=False, debug=False)

    x_in = nc.dram_tensor("x", [S, D], F32, kind="ExternalInput").ap()
    Wqkv = nc.dram_tensor("W_qkv", [3 * D, D], F32, kind="ExternalInput").ap()
    Wout = nc.dram_tensor("W_out", [D, D], F32, kind="ExternalInput").ap()
    gw = nc.dram_tensor("gate_w", [NEXP, D], F32, kind="ExternalInput").ap()
    w1 = nc.dram_tensor("w1", [NEXP, D, H], F32, kind="ExternalInput").ap()
    w2 = nc.dram_tensor("w2", [NEXP, H, D], F32, kind="ExternalInput").ap()
    sw1 = nc.dram_tensor("sw1", [1, D, H], F32, kind="ExternalInput").ap()
    sw2 = nc.dram_tensor("sw2", [1, H, D], F32, kind="ExternalInput").ap()

    def scr(name, shape, dtype, dbg):
        kind = "ExternalOutput" if stage == dbg else "Internal"
        return nc.dram_tensor(name, shape, dtype, kind=kind).ap()

    qd_s = scr("qd_s", [NH, 128, S], BF16, 1)
    kd_s = scr("kd_s", [NH, 128, S], BF16, 1)
    vd_s = scr("vd_s", [NH, 128, S], BF16, 1)
    ao_s = scr("ao_s", [NH, 128, S], BF16, 2)
    x1f_s = scr("x1f_s", [SB, 128, D], F32, 3)
    x1t_s = scr("x1t_s", [2 * NSTRIPE, 128, KD * 512], BF16, 3)
    x1r_s = scr("x1r_s", [S, D], BF16, None)
    y_s = nc.dram_tensor("y_s", [S + 128, D], F32, kind="Internal").ap()
    idxs_s = nc.dram_tensor("idxs_s", [3 * NEXP * 128 * 8 + 256, 1],
                            mybir.dt.uint32,
                            kind="ExternalOutput" if dbgtab else "Internal").ap()
    out_t = (nc.dram_tensor("out", [S, D], F32, kind="ExternalOutput").ap()
             if stage >= 4 else None)
    den_dbg = (nc.dram_tensor("den_dbg", [NH, 128, SB], F32,
                              kind="ExternalOutput").ap() if stage == 2 else None)
    kv_dbg = (nc.dram_tensor("kv_dbg", [2, 128, SB * 129], BF16,
                             kind="ExternalOutput").ap() if stage == 2 else None)
    ss_dbg = (nc.dram_tensor("ss_dbg", [128, 512 + 256 + 129], F32,
                             kind="ExternalOutput").ap() if stage == 2 else None)
    g_dbg = (nc.dram_tensor("g_dbg", [3, 128, 128], F32,
                            kind="ExternalOutput").ap() if dbg4 else None)
    rt_dbg = (nc.dram_tensor("rt_dbg", [128, 1280], F32,
                             kind="ExternalOutput").ap() if dbgtab else None)
    rtu_dbg = (nc.dram_tensor("rtu_dbg", [128, 192], mybir.dt.uint32,
                              kind="ExternalOutput").ap() if dbgtab else None)
    y_dbg = (nc.dram_tensor("y_dbg", [128, D], F32,
                            kind="ExternalOutput").ap() if dbg4 else None)

    cos_np, sins_np = _rope_tables()
    maskT_np = (np.arange(384)[None, :] >= (np.arange(128)[:, None] + 128)
                ).astype(np.float32)
    pswap_np = np.zeros((128, 128), dtype=ml_dtypes.bfloat16)
    for i in range(0, 128, 2):
        pswap_np[i, i + 1] = 1.0
        pswap_np[i + 1, i] = 1.0
    c_cos = nc.inline_tensor(cos_np, "c_cos").ap()
    c_sins = nc.inline_tensor(sins_np, "c_sins").ap()
    c_mask = nc.inline_tensor(maskT_np, "c_mask").ap()
    c_psw = nc.inline_tensor(pswap_np, "c_psw").ap()
    c_id = nc.inline_tensor(np.eye(128, dtype=ml_dtypes.bfloat16), "c_id").ap()
    c_cc = nc.inline_tensor(_mk_consts(), "c_cc").ap()
    c_idf = nc.inline_tensor(np.eye(16, dtype=np.float32), "c_idf").ap()
    c_iotaE = nc.inline_tensor(
        np.tile(np.arange(NEXP, dtype=np.float32)[None, :], (128, 1)),
        "c_iotaE").ap()
    ids_np = (np.arange(SB)[None, :] * 128 +
              np.arange(128)[:, None]).astype(np.uint32)
    c_ids = nc.inline_tensor(ids_np, "c_ids").ap()
    padinit_np = (S + np.arange(128, dtype=np.uint32)[:, None] +
                  np.zeros((1, 128), np.uint32))
    c_padi = nc.inline_tensor(padinit_np, "c_padi").ap()
    dump_np = (2.0 * NEXP * 128 * 8 +
               np.arange(128, dtype=np.float32)[:, None])
    c_dump = nc.inline_tensor(dump_np, "c_dump").ap()
    shids_np = np.zeros((128, 32), np.uint32)
    for b_ in range(4):
        for jj in range(8):
            shids_np[:, b_ * 8 + jj] = b_ * 1024 + jj * 128 + np.arange(128)
    c_shids = nc.inline_tensor(shids_np, "c_shids").ap()

    ctx = ExitStack()
    with ctx:
        arena = ctx.enter_context(nc.sbuf_tensor("arena", [128, ARENA], BF16))

        def bfv(off, n):
            return arena[:, off:off + n]

        def f32v(off, n):
            return arena[:, off:off + 2 * n].bitcast(F32)

        ps = [ctx.enter_context(nc.psum_tensor(f"ps{i}", [128, 512], F32))
              for i in range(8)]

        def ps_bf(b, t):
            return ps[b][:, t * 64:(t + 1) * 64].bitcast(BF16)

        def ps_bf8(b):
            return ps[b][:, 0:512].bitcast(BF16).rearrange(
                "p (t d) -> p t d", d=128)

        p = Prog()
        env = {}
        for sname in ("d_g", "d_s", "m_pe", "m_act", "m_dve", "m_pool"):
            p.sem(sname)
            env[sname] = ctx.enter_context(nc.semaphore(sname))
        block = ctx.enter_context(nc.Block())

        M = {}

        def mark(name, sem):
            M[name] = (sem, p.cnt[sem])

        def wm(eng, name):
            s_, v_ = M[name]
            p.wait(eng, s_, v_)

        A_COS, A_SINS, A_MASK, A_PSW, A_ID = 0, 4096, 8192, 8960, 9088
        A_GATET, A_CCOL = 102400, 102800
        cosv, sinsv = bfv(A_COS, S), bfv(A_SINS, S)
        maskv = f32v(A_MASK, 384)
        pswv, idv = bfv(A_PSW, 128), bfv(A_ID, 128)
        gateTv3 = bfv(A_GATET, KD * NEXP).rearrange("p (k e) -> p k e", e=NEXP)
        ccol = f32v(A_CCOL, NCONST)

        def cc(j):
            return ccol[:, j:j + 1]

        p.dma("gpsimd", cosv, c_cos)
        p.dma("gpsimd", sinsv, c_sins)
        p.dma("gpsimd", maskv, c_mask)
        p.dma("gpsimd", pswv, c_psw)
        p.dma("gpsimd", idv, c_id)
        p.dma("gpsimd", ccol, c_cc)
        A_GWB = 9600
        gwb = bfv(A_GWB, D)
        p.dma("gpsimd", gwb[0:NEXP, :], gw)
        mark("p0_loads", "d_g")
        wm("tensor", "p0_loads")
        for half in range(2):
            bank = 6 + half
            p.pe_bank(bank)
            for t in range(8):
                k = half * 8 + t
                p.tr(ps_bf(bank, t)[:, 0:NEXP],
                     gwb[0:NEXP, k * 128:(k + 1) * 128],
                     idv[0:NEXP, 0:NEXP], inc="m_pe" if t == 7 else None)
            p.wait("scalar", "m_pe")
            p.acopy(gateTv3[:, half * 8:half * 8 + 8, :],
                    ps_bf8(bank)[:, :, 0:NEXP], inc="m_act")
            p.bank_read(bank, "m_act")
        mark("gateT", "m_act")

        A_XT, A_WB, A_WT, A_XB, A_QPL = 9600, 75136, 79232, 83328, 87424
        A_R1, A_R2, A_R3 = 95616, 97664, 99712
        xTv = [bfv(A_XT + k * S, S) for k in range(KD)]
        xT8 = [bfv(A_XT + h8 * 8 * S, 8 * S).rearrange("p (t s) -> p t s", s=S)
               for h8 in range(2)]
        wbv = [bfv(A_WB + i * D, D) for i in range(2)]
        wTv = [bfv(A_WT + i * D, D) for i in range(2)]
        xbv = [bfv(A_XB + i * D, D) for i in range(2)]
        qplv = [bfv(A_QPL + i * S, S) for i in range(2)]
        r1p = [f32v(A_R1 + i * 1024, 512) for i in range(2)]
        r2p = [f32v(A_R2 + i * 1024, 512) for i in range(2)]
        r3p = [f32v(A_R3 + i * 1024, 512) for i in range(2)]

        for sbi in range(SB):
            if sbi >= 2:
                wm("gpsimd", f"p1a_tr{sbi - 2}")
            p.dma("gpsimd", xbv[sbi % 2], x_in[sbi * 128:(sbi + 1) * 128, :])
            xl = p.cnt["d_g"]
            p.wait("tensor", "d_g", xl)
            for half in range(2):
                bank = 6 + half
                p.pe_bank(bank)
                for t in range(8):
                    k = half * 8 + t
                    p.tr(ps_bf(bank, t),
                         xbv[sbi % 2][:, k * 128:(k + 1) * 128], idv,
                         inc="m_pe" if t == 7 else None)
                if half == 1:
                    mark(f"p1a_tr{sbi}", "m_pe")
                p.wait("scalar", "m_pe")
                p.acopy(xT8[half][:, :, sbi * 128:sbi * 128 + 128],
                        ps_bf8(bank), inc="m_act")
                p.bank_read(bank, "m_act")
        mark("p1a_done", "m_act")

        GROUPS = [(0, 1, 2), (3, 4, 5), (6, 7)]
        GBANKS = [(0, 1, 2), (3, 4, 5), (0, 1)]
        wload = []
        p.dma("gpsimd", wbv[0], Wqkv[0:128, :])
        wload.append(p.cnt["d_g"])
        p.dma("gpsimd", wbv[1], Wqkv[128:256, :])
        wload.append(p.cnt["d_g"])
        wm("tensor", "p1a_done")

        for jb in range(48):
            which, head = jb // NH, jb % NH
            buf = jb % 2
            p.wait("tensor", "d_g", wload[jb])
            for half in range(2):
                bank = 6 + half
                p.pe_bank(bank)
                for t in range(8):
                    k = half * 8 + t
                    p.tr(ps_bf(bank, t),
                         wbv[buf][:, k * 128:(k + 1) * 128], idv,
                         inc="m_pe" if t == 7 else None)
                if half == 1:
                    mark(f"wtr{jb}", "m_pe")
                p.wait("scalar", "m_pe")
                p.acopy(bfv(A_WT + buf * D + half * 1024, 1024).rearrange(
                    "p (t d) -> p t d", d=128), ps_bf8(bank), inc="m_act")
                p.bank_read(bank, "m_act")
            mark(f"wtrev{jb}", "m_act")
            if jb + 2 < 48:
                wm("gpsimd", f"wtr{jb}")
                p.dma("gpsimd", wbv[buf], Wqkv[(jb + 2) * 128:(jb + 3) * 128, :])
                wload.append(p.cnt["d_g"])
            if jb >= 2:
                wm("scalar", f"spill{jb - 2}")
            wm("tensor", f"wtrev{jb}")
            for gi, grp in enumerate(GROUPS):
                banks = GBANKS[gi]
                for b in banks:
                    p.pe_bank(b)
                for k in range(KD):
                    lhsT = wTv[buf][:, k * 128:(k + 1) * 128]
                    for ci, c in enumerate(grp):
                        p.mm(ps[banks[ci]][:, :], lhsT,
                             xTv[k][:, c * 512:(c + 1) * 512],
                             start=(k == 0), stop=(k == KD - 1),
                             inc=("m_pe" if (k == KD - 1 and
                                             ci == len(grp) - 1) else None))
                mark(f"qkvmm{jb}_{gi}", "m_pe")
                wm("scalar", f"qkvmm{jb}_{gi}")
                for ci, c in enumerate(grp):
                    p.acopy(qplv[buf][:, c * 512:(c + 1) * 512],
                            ps[banks[ci]][:, :], inc="m_act")
                    p.bank_read(banks[ci], "m_act")
                    mark(f"qkvev{jb}_{c}", "m_act")
            if which < 2:
                for c in range(8):
                    par = c % 2
                    bank = 6 + par
                    cs = slice(c * 512, (c + 1) * 512)
                    p.pe_bank(bank)
                    wm("tensor", f"qkvev{jb}_{c}")
                    p.mm(ps[bank][:, :], pswv, qplv[buf][:, cs],
                         start=True, stop=True, inc="m_pe")
                    mark(f"psw{jb}_{c}", "m_pe")
                    if c >= 2:
                        wm("vector", f"rope2_{jb}_{c - 2}")
                    wm("vector", f"psw{jb}_{c}")
                    p.tt(r2p[par], ps[bank][:, :], sinsv[:, cs], ALU.mult,
                         inc="m_dve")
                    p.bank_read(bank, "m_dve")
                    p.tt(r1p[par], qplv[buf][:, cs], cosv[:, cs], ALU.mult)
                    p.tt(r1p[par], r1p[par], r2p[par], ALU.add, inc="m_dve")
                    mark(f"rope1_{jb}_{c}", "m_dve")
                    wm("scalar", f"rope1_{jb}_{c}")
                    p.act(r2p[par], r1p[par], AF.Exp)
                    p.act(r3p[par], r1p[par], AF.Relu, inc="m_act")
                    mark(f"rope2_{jb}_{c}", "m_act")
                    wm("vector", f"rope2_{jb}_{c}")
                    p.ts(r2p[par], r2p[par], 1.0, ALU.min)
                    p.tt(r1p[par], r2p[par], r3p[par], ALU.add)
                    p.vcopy(qplv[buf][:, cs], r1p[par], inc="m_dve")
                    mark(f"ropewb{jb}_{c}", "m_dve")
                wm("sync", f"ropewb{jb}_{7}")
                p.dma("sync", (qd_s if which == 0 else kd_s)[head],
                      qplv[buf], inc="d_s")
            else:
                wm("sync", f"qkvev{jb}_{7}")
                p.dma("sync", vd_s[head], qplv[buf], inc="d_s")
            mark(f"spill{jb}", "d_s")
        mark("p1_pe_done", "m_pe")
        mark("p1_act_done", "m_act")
        mark("p1_dve_done", "m_dve")
        if stage == 1:
            _finish(p, env, block)
            return nc

        A_QH, A_KH, A_VH = 9600, 17792, 25984
        A_KS, A_VE = 34176, 38272
        A_ATT, A_SSB = 42400, 43424
        A_NUMF, A_DEN, A_DENR = 43568, 51760, 51824
        A_AOB, A_AOPL = 51888, 52912
        A_WOT, A_WOB = 61104, 93872
        qhv = [bfv(A_QH + i * S, S) for i in range(2)]
        khv = [bfv(A_KH + i * S, S) for i in range(2)]
        vhv = [bfv(A_VH + i * S, S) for i in range(2)]
        ksv3 = bfv(A_KS, SB * 128).rearrange("p (t d) -> p t d", d=128)
        vev_flat = bfv(A_VE, SB * 129)
        vev3 = vev_flat.rearrange("p (t d) -> p t d", d=129)
        attv2 = [bfv(A_ATT, 512), bfv(A_ATT + 512, 512)]
        ssbv = bfv(A_SSB, 129)
        numfv = f32v(A_NUMF, SB * 128)
        denv, denrv = f32v(A_DEN, SB), f32v(A_DENR, SB)
        aobv = bfv(A_AOB, 1024)
        aoplv = [bfv(A_AOPL + i * S, S) for i in range(2)]
        woTv = [bfv(A_WOT + k * D, D) for k in range(KD)]
        wobv = [bfv(A_WOB + i * D, D) for i in range(2)]

        wm("gpsimd", "p1_pe_done")
        wm("gpsimd", "p1_act_done")
        wm("gpsimd", "p1_dve_done")
        wm("gpsimd", f"spill{47}")
        p.memset("gpsimd", vev_flat, 1.0, inc="m_pool")
        mark("ve_ones", "m_pool")

        hload = []
        for hh in range(2):
            for src, dstb in ((qd_s, qhv), (kd_s, khv), (vd_s, vhv)):
                p.dma("gpsimd", dstb[hh], src[hh])
            hload.append(p.cnt["d_g"])

        for h_i in range(NH):
            buf = h_i % 2
            p.wait("tensor", "d_g", hload[h_i])
            for src, kind in ((khv[buf], "ks"), (vhv[buf], "ve")):
                for batch in range(4):
                    bank = 6 + (batch % 2)
                    p.pe_bank(bank)
                    for t in range(8):
                        tt_ = batch * 8 + t
                        p.tr(ps_bf(bank, t),
                             src[:, tt_ * 128:(tt_ + 1) * 128], idv,
                             inc="m_pe" if t == 7 else None)
                    p.wait("scalar", "m_pe")
                    if kind == "ve":
                        wm("scalar", "ve_ones")
                        dest = vev3[:, batch * 8:batch * 8 + 8, 0:128]
                    else:
                        dest = ksv3[:, batch * 8:batch * 8 + 8, :]
                    p.acopy(dest, ps_bf8(bank), inc="m_act")
                    p.bank_read(bank, "m_act")
            mark(f"kv_tr{h_i}", "m_act")
            if stage == 2 and h_i == 0:
                wm("sync", f"kv_tr{h_i}")
                p.dma("sync", kv_dbg[0][:, 0:SB * 128],
                      bfv(A_KS, SB * 128), inc="d_s")
                p.dma("sync", kv_dbg[1], vev_flat, inc="d_s")
                mark("kvdbg", "d_s")
            if h_i > 0:
                wm("vector", f"aocast{h_i - 1}")
                wm("tensor", f"amask{h_i - 1}_{NCH - 1}")
                wm("tensor", f"numev{h_i - 1}_{NCH - 1}")
                wm("tensor", f"scopy{h_i - 1}_{NCH - 1}")
            wm("tensor", f"kv_tr{h_i}")
            for b in (1, 2, 3):
                p.pe_bank(b)

            def att_group(c, h_i=h_i, buf=buf):
                bank = 0 if c % 2 == 0 else 4
                s0 = c * CH
                p.pe_bank(bank)
                p.mm(ps[bank][:, 0:256], khv[buf][:, s0:s0 + 128],
                     qhv[buf][:, s0:s0 + 256], start=True, stop=True)
                p.mm(ps[bank][:, 256:512], khv[buf][:, s0 + 128:s0 + 256],
                     qhv[buf][:, s0:s0 + 256], start=True, stop=True,
                     inc="m_pe")
                mark(f"att{h_i}_{c}", "m_pe")
                wm("vector", f"att{h_i}_{c}")
                if c >= 2:
                    wm("vector", f"num{h_i}_{c - 2}")
                av = attv2[c % 2]
                p.tt(av[:, 0:256], ps[bank][:, 0:256], maskv[:, 128:384],
                     ALU.mult)
                p.tt(av[:, 256:512], ps[bank][:, 256:512], maskv[:, 0:256],
                     ALU.mult, inc="m_dve")
                p.bank_read(bank, "m_dve")
                mark(f"amask{h_i}_{c}", "m_dve")

            def num_group(c, h_i=h_i, buf=buf):
                s0 = c * CH
                av = attv2[c % 2]
                wm("tensor", f"amask{h_i}_{c}")
                if c > 0:
                    wm("tensor", f"scopy{h_i}_{c}")
                    wm("tensor", f"numev{h_i}_{c - 1}")
                for lh in range(2):
                    psn = ps[1 + lh]
                    lc = lh * 128
                    p.mm(psn[:, 0:129], av[:, lc:lc + 128],
                         vev3[:, 2 * c, :], start=True, stop=False)
                    if c == 0:
                        p.mm(psn[:, 0:129], av[:, 256 + lc:256 + lc + 128],
                             vev3[:, 2 * c + 1, :], start=False, stop=True,
                             inc=("m_pe" if lh == 1 else None))
                    else:
                        p.mm(psn[:, 0:129], av[:, 256 + lc:256 + lc + 128],
                             vev3[:, 2 * c + 1, :], start=False, stop=False)
                        p.mm(psn[:, 0:129], qhv[buf][:, s0 + lc:s0 + lc + 128],
                             ssbv[:, 0:129], start=False, stop=True,
                             inc=("m_pe" if lh == 1 else None))
                mark(f"num{h_i}_{c}", "m_pe")
                p.mm(ps[3][:, 0:129], ksv3[:, 2 * c, :], vev3[:, 2 * c, :],
                     start=(c == 0), stop=False)
                p.mm(ps[3][:, 0:129], ksv3[:, 2 * c + 1, :],
                     vev3[:, 2 * c + 1, :], start=False, stop=(c == NCH - 1),
                     inc="m_pe")
                mark(f"upd{h_i}_{c}", "m_pe")
                if c < NCH - 1:
                    wm("scalar", f"upd{h_i}_{c}")
                    p.acopy(ssbv[:, 0:129], ps[3][:, 0:129], inc="m_act")
                    mark(f"scopy{h_i}_{c + 1}", "m_act")
                wm("vector", f"num{h_i}_{c}")
                for lh in range(2):
                    i_ = 2 * c + lh
                    p.vcopy(numfv[:, i_ * 128:(i_ + 1) * 128],
                            ps[1 + lh][:, 0:128])
                    p.vcopy(denv[:, i_:i_ + 1], ps[1 + lh][:, 128:129],
                            inc="m_dve")
                mark(f"numev{h_i}_{c}", "m_dve")

            att_group(0)
            for c in range(NCH):
                if c + 1 < NCH:
                    att_group(c + 1)
                num_group(c)
            p.vfence()
            p.ts(denv[:, :], denv[:, :], 1e-6, ALU.max, inc="m_dve")
            p.vfence()
            p.recip(denrv[:, :], denv[:, :], inc="m_dve")
            p.vfence()
            for i_ in range(SB):
                p.ts(numfv[:, i_ * 128:(i_ + 1) * 128],
                     numfv[:, i_ * 128:(i_ + 1) * 128],
                     denrv[:, i_:i_ + 1], ALU.mult,
                     inc="m_dve" if i_ == SB - 1 else None)
            mark(f"aoscale{h_i}", "m_dve")
            if stage == 2 and den_dbg is not None:
                wm("sync", f"aoscale{h_i}")
                p.dma("sync", den_dbg[h_i], denv, inc="d_s")
            if h_i >= 2:
                wm("scalar", f"aospill{h_i - 2}")
            wm("scalar", f"aoscale{h_i}")
            for batch in range(4):
                bank = 6 + (batch % 2)
                p.acopy(aobv[:, :],
                        numfv[:, batch * 1024:(batch + 1) * 1024],
                        inc="m_act")
                mark(f"aocastb{h_i}_{batch}", "m_act")
                p.pe_bank(bank)
                wm("tensor", f"aocastb{h_i}_{batch}")
                for t in range(8):
                    p.tr(ps_bf(bank, t), aobv[:, t * 128:(t + 1) * 128], idv,
                         inc="m_pe" if t == 7 else None)
                p.wait("scalar", "m_pe")
                p.acopy(aoplv[buf][:, :].rearrange("p (t s) -> p t s", s=128)
                        [:, batch * 8:batch * 8 + 8, :],
                        ps_bf8(bank), inc="m_act")
                p.bank_read(bank, "m_act")
            mark(f"aocast{h_i}", "m_act")
            wm("sync", f"aocast{h_i}")
            p.dma("sync", ao_s[h_i], aoplv[buf], inc="d_s")
            mark(f"aospill{h_i}", "d_s")
            if h_i + 2 < NH:
                wm("gpsimd", f"num{h_i}_{NCH - 1}")
                for src, dstb in ((qd_s, qhv), (kd_s, khv), (vd_s, vhv)):
                    p.dma("gpsimd", dstb[buf], src[h_i + 2])
                hload.append(p.cnt["d_g"])
            if h_i >= 2:
                wm("gpsimd", f"wotr{h_i - 2}")
            p.dma("gpsimd", wobv[buf], Wout[h_i * 128:(h_i + 1) * 128, :])
            wol = p.cnt["d_g"]
            p.wait("tensor", "d_g", wol)
            for half in range(2):
                bank = 6 + half
                p.pe_bank(bank)
                for t in range(8):
                    k = half * 8 + t
                    p.tr(ps_bf(bank, t),
                         wobv[buf][:, k * 128:(k + 1) * 128], idv,
                         inc="m_pe" if t == 7 else None)
                if half == 1:
                    mark(f"wotr{h_i}", "m_pe")
                p.wait("scalar", "m_pe")
                p.acopy(bfv(A_WOT + half * 8 * D, 8 * D).rearrange(
                    "p (t s) -> p t s", s=D)
                    [:, :, h_i * 128:h_i * 128 + 128],
                    ps_bf8(bank), inc="m_act")
                p.bank_read(bank, "m_act")
        mark("p2_pe_done", "m_pe")
        mark("p2_act_done", "m_act")
        mark("p2_dve_done", "m_dve")
        if stage == 2:
            _finish(p, env, block)
            return nc

        A_AOT, A_XRES, A_TF = 9600, 13696, 21888
        A_SQF, A_X1B, A_X1T, A_ST = 25984, 30080, 32128, 36224
        aotv = [bfv(A_AOT + i * D, D).rearrange("p (k d) -> p k d", d=128)
                for i in range(2)]
        xresv = [f32v(A_XRES + i * 2 * D, D) for i in range(2)]
        tfv = f32v(A_TF, D)
        sqfv = f32v(A_SQF, D)
        x1bv = bfv(A_X1B, D)
        x1tv = [bfv(A_X1T + i * D, D) for i in range(2)]
        stv = f32v(A_ST, 16)
        ST_SUM, ST_SQ, ST_MEAN, ST_Q1, ST_M2, ST_VAR, ST_RSTD, ST_NMR = range(8)

        def st(j):
            return stv[:, j:j + 1]

        def u32v(off, n):
            return arena[:, off:off + 2 * n].bitcast(UI32)

        scoresv = f32v(0, 512)
        combv = f32v(1024, 512)
        m1mv = f32v(2048, 512)
        m2xv = f32v(3072, 512)
        rkTv = f32v(4096, 512)
        c1v, c2v = f32v(5120, 32), f32v(5184, 32)
        c1uv, c2uv = u32v(5120, 32), u32v(5184, 32)
        e1v, e2v = f32v(5248, 32), f32v(5312, 32)
        rk1v, rk2v = f32v(5376, 32), f32v(5440, 32)
        offf1, offf2 = f32v(5504, 32), f32v(5568, 32)
        offu1, offu2 = u32v(5632, 32), u32v(5696, 32)
        offc1, offc2 = u32v(7040, 32), u32v(7104, 32)
        tmpav, tmpbv = f32v(5760, 32), f32v(5824, 32)
        gsv = f32v(5888, 8)
        mskv = f32v(5904, 16)
        m2v = f32v(5936, 16)
        t12v = f32v(5968, 16)
        onesc8 = f32v(6000, 8)
        c_iotaEv = f32v(6016, 16)
        mskbfv = bfv(6048, 512)
        idcolv = u32v(6560, 32)
        idfv = arena[0:16, 6624:6656].bitcast(F32)
        shidtv = u32v(6656, 32)
        idxtv = [u32v(6720, 8), u32v(6736, 8)]
        cvtv = [u32v(6752, 8), u32v(6768, 8)]
        cvtfv = [f32v(6752, 8), f32v(6768, 8)]
        hugev = u32v(6784, 128)
        gidxtv = [u32v(7168, 8), u32v(7184, 8)]
        gtmpv = f32v(7200, 8)
        dumpc = f32v(7232, 1)

        wm("gpsimd", "p2_pe_done")
        wm("gpsimd", "p2_act_done")
        wm("gpsimd", "p2_dve_done")
        wm("gpsimd", f"aospill{NH - 1}")
        p.dma("gpsimd", c_iotaEv, c_iotaE)
        p.dma("gpsimd", idfv, c_idf)
        p.dma("gpsimd", idcolv, c_ids)
        p.dma("gpsimd", shidtv, c_shids)
        p.dma("gpsimd", dumpc, c_dump[:, 0:1])
        mark("routec", "d_g")
        wm("vector", "routec")
        p.memset("vector", onesc8, 1.0, inc="m_dve")
        wm("tensor", "routec")
        aol = []
        for i in range(2):
            src = ao_s[:, :, i * 128:(i + 1) * 128].transpose([1, 0, 2])
            p.dma("gpsimd", aotv[i][:, 0:8, :], src[:, 0:8, :])
            p.dma("gpsimd", aotv[i][:, 8:16, :], src[:, 8:16, :])
            p.dma("gpsimd", xresv[i], x_in[i * 128:(i + 1) * 128, :])
            aol.append(p.cnt["d_g"])
        for sbi in range(SB):
            buf = sbi % 2
            p.wait("tensor", "d_g", aol[sbi])
            for b in range(4):
                p.pe_bank(b)
            for k in range(KD):
                lhsT = aotv[buf][:, k, :]
                for ec in range(4):
                    p.mm(ps[ec][:, :], lhsT,
                         woTv[k][:, ec * 512:(ec + 1) * 512],
                         start=(k == 0), stop=(k == KD - 1),
                         inc=("m_pe" if (k == KD - 1 and ec == 3) else None))
            mark(f"p3mm{sbi}", "m_pe")
            if sbi > 0:
                wm("vector", f"p3norm{sbi - 1}")
            wm("vector", f"p3mm{sbi}")
            for ec in range(4):
                p.tt(tfv[:, ec * 512:(ec + 1) * 512], ps[ec][:, :],
                     xresv[buf][:, ec * 512:(ec + 1) * 512], ALU.add,
                     inc="m_dve")
                p.bank_read(ec, "m_dve")
            p.rsum(st(ST_SUM), tfv[:, :], inc="m_dve")
            mark(f"p3t{sbi}", "m_dve")
            p.vfence()
            if sbi > 0:
                wm("scalar", f"p3spill{sbi - 1}")
            wm("scalar", f"p3t{sbi}")
            p.act(sqfv[:, :], tfv[:, :], AF.Square, accum=st(ST_SQ),
                  inc="m_act")
            mark(f"p3sq{sbi}", "m_act")
            wm("vector", f"p3sq{sbi}")
            p.ts(st(ST_MEAN), st(ST_SUM), cc(C_INVD), ALU.mult, inc="m_dve")
            p.ts(st(ST_Q1), st(ST_SQ), cc(C_INVD), ALU.mult, inc="m_dve")
            p.vfence()
            p.tt(st(ST_M2), st(ST_MEAN), st(ST_MEAN), ALU.mult, inc="m_dve")
            p.vfence()
            p.tt(st(ST_VAR), st(ST_Q1), st(ST_M2), ALU.subtract, inc="m_dve")
            mark(f"p3var{sbi}", "m_dve")
            wm("scalar", f"p3var{sbi}")
            p.act(st(ST_Q1), st(ST_VAR), AF.Ln, bias=cc(C_EPS), inc="m_act")
            p.afence()
            p.act(st(ST_RSTD), st(ST_Q1), AF.Exp, scale=cc(C_NEGHALF),
                  inc="m_act")
            mark(f"p3rstd{sbi}", "m_act")
            wm("vector", f"p3rstd{sbi}")
            p.tt(st(ST_NMR), st(ST_MEAN), st(ST_RSTD), ALU.mult, inc="m_dve")
            p.vfence()
            p.ts(st(ST_NMR), st(ST_NMR), -1.0, ALU.mult, inc="m_dve")
            mark(f"p3nmr{sbi}", "m_dve")
            wm("scalar", f"p3nmr{sbi}")
            p.act(sqfv[:, :], tfv[:, :], AF.Identity, scale=st(ST_RSTD),
                  bias=st(ST_NMR), inc="m_act")
            mark(f"p3norm{sbi}", "m_act")
            p.acopy(x1bv[:, :], sqfv[:, :], inc="m_act")
            mark(f"p3x1b{sbi}", "m_act")
            wm("sync", f"p3x1b{sbi}")
            p.dma("sync", x1f_s[sbi], sqfv, inc="d_s")
            wm("tensor", f"p3x1b{sbi}")
            for half in range(2):
                bank = 4 + half
                p.pe_bank(bank)
                for t in range(8):
                    k = half * 8 + t
                    p.tr(ps_bf(bank, t), x1bv[:, k * 128:(k + 1) * 128], idv,
                         inc="m_pe" if t == 7 else None)
                p.wait("scalar", "m_pe")
                p.acopy(x1tv[buf][:, :].rearrange("p (k d) -> p k d", d=128)
                        [:, half * 8:half * 8 + 8, :],
                        ps_bf8(bank), inc="m_act")
                p.bank_read(bank, "m_act")
            mark(f"p3x1t{sbi}", "m_act")
            p.dma("sync", x1r_s[sbi * 128:(sbi + 1) * 128, :], x1bv,
                  inc="d_s")
            mark(f"p3spill{sbi}", "d_s")
            wm("tensor", f"p3x1t{sbi}")
            p.pe_bank(6)
            for k in range(KD):
                p.mm(ps[6][:, 0:NEXP],
                     x1tv[buf][:, k * 128:(k + 1) * 128],
                     gateTv3[:, k, :],
                     start=(k == 0), stop=(k == KD - 1),
                     inc=("m_pe" if k == KD - 1 else None))
            mark(f"gmm{sbi}", "m_pe")
            wm("scalar", f"gmm{sbi}")
            p.act(scoresv[:, sbi * NEXP:(sbi + 1) * NEXP], ps[6][:, 0:NEXP],
                  AF.Sigmoid, inc="m_act")
            p.bank_read(6, "m_act")
            mark(f"gsc{sbi}", "m_act")
            wm("vector", f"gsc{sbi}")
            sv = scoresv[:, sbi * NEXP:(sbi + 1) * NEXP]
            cmb = combv[:, sbi * NEXP:(sbi + 1) * NEXP]
            m1c = m1mv[:, sbi * NEXP:(sbi + 1) * NEXP]
            m2c = m2xv[:, sbi * NEXP:(sbi + 1) * NEXP]
            for j, (a, b) in enumerate(PAIRS):
                p.tt(gsv[:, 4:8], sv[:, a:16:4], sv[:, b:16:4], ALU.add,
                     inc="m_dve")
                p.vfence()
                if j == 0:
                    p.vcopy(gsv[:, 0:4], gsv[:, 4:8], inc="m_dve")
                else:
                    p.tt(gsv[:, 0:4], gsv[:, 0:4], gsv[:, 4:8], ALU.max,
                         inc="m_dve")
                p.vfence()
            p.tt(mskv[:, 0:2], gsv[:, 0:2], gsv[:, 2:4], ALU.max, inc="m_dve")
            p.tt(mskv[:, 2:4], gsv[:, 0:2], gsv[:, 2:4], ALU.min, inc="m_dve")
            p.vfence()
            p.tt(mskv[:, 4:5], mskv[:, 0:1], mskv[:, 1:2], ALU.min,
                 inc="m_dve")
            p.tt(mskv[:, 5:6], mskv[:, 2:3], mskv[:, 3:4], ALU.max,
                 inc="m_dve")
            p.vfence()
            p.tt(mskv[:, 6:7], mskv[:, 4:5], mskv[:, 5:6], ALU.max,
                 inc="m_dve")
            p.vfence()
            p.ts(gsv[:, 0:4], gsv[:, 0:4], mskv[:, 6:7], ALU.is_ge,
                 inc="m_dve")
            p.vfence()
            for i4 in range(4):
                p.tt(cmb[:, i4:16:4], sv[:, i4:16:4], gsv[:, 0:4], ALU.mult,
                     inc="m_dve")
            p.vfence()
            p.rmax(mskv[:, 8:9], cmb, inc="m_dve")
            p.vfence()
            p.ts(m1c, cmb, mskv[:, 8:9], ALU.is_ge, inc="m_dve")
            p.vfence()
            p.stt(m2v[:, 0:16], m1c, -1e30, cmb, ALU.mult, ALU.add,
                  inc="m_dve")
            p.vfence()
            p.rmax(mskv[:, 9:10], m2v[:, 0:16], inc="m_dve")
            p.vfence()
            p.tt(t12v[:, 0:1], mskv[:, 8:9], mskv[:, 9:10], ALU.add,
                 inc="m_dve")
            p.ts(m2c, cmb, mskv[:, 9:10], ALU.is_ge, inc="m_dve")
            p.vfence()
            p.ts(t12v[:, 0:1], t12v[:, 0:1], 1e-20, ALU.add, inc="m_dve")
            p.tt(m2c, m2c, m1c, ALU.subtract, inc="m_dve")
            p.vfence()
            p.tt(mskv[:, 0:16], m1c, m2c, ALU.add, inc="m_dve")
            p.vfence()
            p.tt(cmb, cmb, mskv[:, 0:16], ALU.mult, inc="m_dve")
            p.ts(t12v[:, 0:8], onesc8[:, :], t12v[:, 0:1], ALU.mult,
                 inc="m_dve")
            p.vfence()
            p.recip(t12v[:, 8:16], t12v[:, 0:8], inc="m_dve")
            p.vfence()
            p.ts(cmb, cmb, t12v[:, 8:9], ALU.mult, inc="m_dve")
            p.vfence()
            p.tt(m2v[:, 0:16], cmb, m1c, ALU.mult, inc="m_dve")
            p.vfence()
            p.rsum(c1v[:, sbi:sbi + 1], m2v[:, 0:16], inc="m_dve")
            p.tt(m2v[:, 0:16], cmb, m2c, ALU.mult, inc="m_dve")
            p.vfence()
            p.rsum(c2v[:, sbi:sbi + 1], m2v[:, 0:16], inc="m_dve")
            p.tt(m2v[:, 0:16], c_iotaEv[:, :], m1c, ALU.mult, inc="m_dve")
            p.vfence()
            p.rsum(e1v[:, sbi:sbi + 1], m2v[:, 0:16], inc="m_dve")
            p.tt(m2v[:, 0:16], c_iotaEv[:, :], m2c, ALU.mult, inc="m_dve")
            p.vfence()
            p.rsum(e2v[:, sbi:sbi + 1], m2v[:, 0:16], inc="m_dve")
            p.vfence()
            mark(f"p3route{sbi}", "m_dve")
            if sbi + 2 < SB:
                wm("gpsimd", f"p3mm{sbi}")
                wm("gpsimd", f"p3t{sbi}")
                nx = sbi + 2
                src = ao_s[:, :, nx * 128:(nx + 1) * 128].transpose([1, 0, 2])
                p.dma("gpsimd", aotv[buf][:, 0:8, :], src[:, 0:8, :])
                p.dma("gpsimd", aotv[buf][:, 8:16, :], src[:, 8:16, :])
                p.dma("gpsimd", xresv[buf], x_in[nx * 128:(nx + 1) * 128, :])
                aol.append(p.cnt["d_g"])
        mark("p3_pe_done", "m_pe")
        mark("p3_act_done", "m_act")
        mark("p3_dve_done", "m_dve")
        mark("p3_ds_done", "d_s")
        if stage == 3:
            _finish(p, env, block)
            return nc

        CAP, CAPJ = 1024, 8
        A_XG, A_XET, A_H4 = 9600, 17792, 34176
        A_W14, A_W24, A_YEV = 45440, 67968, 90496
        A_MSKT, A_CUM, A_ONES = 17792, 25984, 34176
        xgv = [bfv(A_XG + i * D, D) for i in range(2)]
        xet3 = bfv(A_XET, KD * CAP).rearrange("p (k s) -> p k s", s=CAP)
        h4v = bfv(A_H4, KH * CAP).rearrange("p (k s) -> p k s", s=CAP)
        w14 = bfv(A_W14, KD * H).rearrange("p (k h) -> p k h", h=H)
        w24 = bfv(A_W24, KH * D).rearrange("p (k d) -> p k d", d=D)
        yevv = [f32v(A_YEV + i * 2 * D, D) for i in range(2)]
        finy = [f32v(A_XG + i * 2 * D, D) for i in range(4)]
        finx = [f32v(A_XG + 8 * D + i * 2 * D, D) for i in range(4)]
        fst = [f32v(A_XG + 16 * D + i * 32, 8) for i in range(4)]
        mskT = arena[0:16, A_MSKT:A_MSKT + 8192].bitcast(F32)
        cumT = arena[0:16, A_CUM:A_CUM + 8192].bitcast(F32)
        onesT = arena[0:16, A_ONES:A_ONES + 8192].bitcast(F32)
        _flat = idxs_s.squeeze(1)
        idxs3 = _flat[0:NEXP * 128 * CAPJ].rearrange(
            "(e p j) -> e p j", p=128, j=CAPJ)
        cvals3 = _flat[NEXP * 128 * CAPJ:2 * NEXP * 128 * CAPJ].rearrange(
            "(e p j) -> e p j", p=128, j=CAPJ)
        idxs2 = idxs_s
        yrows = y_s

        wm("gpsimd", "p3_pe_done")
        wm("gpsimd", "p3_act_done")
        wm("gpsimd", "p3_dve_done")
        wm("gpsimd", "p3_ds_done")
        wm("vector", "p3_ds_done")

        wm("vector", f"p3route{SB - 1}")
        p.ts(mskbfv[:, :], combv[:, :], 0.0, ALU.is_gt, inc="m_dve")
        mark("mskbf", "m_dve")
        for sb_i in range(SB):
            bank = 6 + (sb_i % 2)
            p.pe_bank(bank)
            wm("tensor", "mskbf")
            p.tr(ps[bank][:, 0:64].bitcast(BF16)[0:16, :],
                 mskbfv[:, sb_i * NEXP:(sb_i + 1) * NEXP], idv, inc="m_pe")
            mark(f"mtr{sb_i}", "m_pe")
            wm("scalar", f"mtr{sb_i}")
            p.acopy(mskT[:, sb_i * 128:(sb_i + 1) * 128],
                    ps[bank][:, 0:64].bitcast(BF16)[0:16, :], inc="m_act")
            p.bank_read(bank, "m_act")
        mark("mskT_done", "m_act")
        p.memset("vector", onesT, 1.0, inc="m_dve")
        wm("vector", "mskT_done")
        p.op("vector",
             lambda h, env: h.tensor_tensor_scan(
                 cumT, onesT, mskT, 0.0, ALU.mult, ALU.add), inc="m_dve")
        p.vfence()
        p.tt(cumT, cumT, mskT, ALU.subtract, inc="m_dve")
        mark("rank_done", "m_dve")
        for sb_i in range(SB):
            bank = 6 + (sb_i % 2)
            p.pe_bank(bank)
            wm("tensor", "rank_done")
            p.tr(ps[bank][:, 0:16], cumT[:, sb_i * 128:(sb_i + 1) * 128],
                 idfv[:, :], inc="m_pe")
            mark(f"rtr{sb_i}", "m_pe")
            wm("scalar", f"rtr{sb_i}")
            p.acopy(rkTv[:, sb_i * NEXP:(sb_i + 1) * NEXP],
                    ps[bank][:, 0:16], inc="m_act")
            p.bank_read(bank, "m_act")
            mark(f"rkT{sb_i}", "m_act")
        mbig = f32v(A_XG, 512)
        wm("vector", f"rkT{SB - 1}")
        p.tt(mbig, rkTv, m1mv[:, 0:512], ALU.mult, inc="m_dve")
        p.vfence()
        for sb_i in range(SB):
            p.rsum(rk1v[:, sb_i:sb_i + 1],
                   mbig[:, sb_i * NEXP:(sb_i + 1) * NEXP], inc="m_dve")
        p.vfence()
        p.tt(mbig, rkTv, m2xv[:, 0:512], ALU.mult, inc="m_dve")
        p.vfence()
        for sb_i in range(SB):
            p.rsum(rk2v[:, sb_i:sb_i + 1],
                   mbig[:, sb_i * NEXP:(sb_i + 1) * NEXP], inc="m_dve")
        p.vfence()
        for rkv_, ev_, offf_, offu_, offc_ in (
                (rk1v, e1v, offf1, offu1, offc1),
                (rk2v, e2v, offf2, offu2, offc2)):
            p.ts(tmpav, rkv_, 1.0 / 128, ALU.mult, s2=-0.49609375,
                 op1=ALU.add, inc="m_dve")
            p.vfence()
            p.vcopy(offu_, tmpav, inc="m_dve")
            p.vfence()
            p.vcopy(tmpav, offu_, inc="m_dve")
            p.vfence()
            p.stt(tmpbv, tmpav, -128.0, rkv_, ALU.mult, ALU.add,
                  inc="m_dve")
            p.vfence()
            p.ts(tmpbv, tmpbv, float(CAPJ), ALU.mult, inc="m_dve")
            p.vfence()
            p.tt(tmpbv, tmpbv, tmpav, ALU.add, inc="m_dve")
            p.vfence()
            p.stt(tmpav, ev_, float(CAP), tmpbv, ALU.mult, ALU.add,
                  inc="m_dve")
            p.ts(tmpbv, rkv_, float(CAP), ALU.is_ge, inc="m_dve")
            p.vfence()
            p.ts(offf_, tmpbv, -1.0, ALU.mult, s2=1.0, op1=ALU.add,
                 inc="m_dve")
            p.vfence()
            p.tt(tmpav, tmpav, offf_, ALU.mult, inc="m_dve")
            p.vfence()
            p.stt(tmpav, tmpbv, dumpc[:, 0:1], tmpav, ALU.mult, ALU.add,
                  inc="m_dve")
            p.vfence()
            p.vcopy(offu_, tmpav, inc="m_dve")
            p.ts(tmpav, tmpav, float(NEXP * CAP), ALU.add, inc="m_dve")
            p.vfence()
            p.vcopy(offc_, tmpav, inc="m_dve")
            p.vfence()
        mark("offs_done", "m_dve")
        p.memset("gpsimd", hugev, 0, inc="m_pool")
        p.op("gpsimd", lambda h, env: h.dma_start(
            out=_flat[NEXP * 128 * CAPJ:2 * NEXP * 128 * CAPJ].rearrange(
                "(p j) -> p j", p=128), in_=hugev), inc="d_g", inck=16)
        p.wait("gpsimd", "d_g", p.cnt["d_g"])
        p.dma("gpsimd", hugev, c_padi)
        p.wait("gpsimd", "d_g", p.cnt["d_g"])
        p.op("gpsimd", lambda h, env: h.dma_start(
            out=_flat[0:NEXP * 128 * CAPJ].rearrange("(p j) -> p j", p=128),
            in_=hugev), inc="d_g", inck=16)
        init_v = p.cnt["d_g"]
        p.wait("gpsimd", "d_g", init_v)
        wm("gpsimd", "offs_done")
        for sb_i in range(SB):
            for offu_, offc_, cv_ in ((offu1, offc1, c1uv),
                                      (offu2, offc2, c2uv)):
                p.op("gpsimd",
                     lambda h, env, o=offu_[:, sb_i:sb_i + 1],
                     i=idcolv[:, sb_i:sb_i + 1]:
                     h.indirect_dma_start(
                         idxs2, IndirectOffsetOnAxis(o, 0), i, None),
                     inc="d_g", inck=16)
                p.op("gpsimd",
                     lambda h, env, o=offc_[:, sb_i:sb_i + 1],
                     i=cv_[:, sb_i:sb_i + 1]:
                     h.indirect_dma_start(
                         idxs2, IndirectOffsetOnAxis(o, 0), i, None),
                     inc="d_g", inck=16)
        mark("idx_done", "d_g")
        if dbgtab:
            wm("sync", "offs_done")
            p.dma("sync", rt_dbg[:, 0:512], scoresv, inc="d_s")
            p.dma("sync", rt_dbg[:, 512:1024], combv, inc="d_s")
            p.dma("sync", rt_dbg[:, 1024:1056], offf1, inc="d_s")
            p.dma("sync", rt_dbg[:, 1056:1088], offf2, inc="d_s")
            p.dma("sync", rt_dbg[:, 1088:1120], rk1v, inc="d_s")
            p.dma("sync", rt_dbg[:, 1120:1152], rk2v, inc="d_s")
            p.dma("sync", rt_dbg[:, 1152:1184], e1v, inc="d_s")
            p.dma("sync", rt_dbg[:, 1184:1216], e2v, inc="d_s")
            p.dma("sync", rt_dbg[:, 1216:1248], c1v, inc="d_s")
            p.dma("sync", rt_dbg[:, 1248:1280], c2v, inc="d_s")
            p.dma("sync", rtu_dbg[:, 0:32], offu1, inc="d_s")
            p.dma("sync", rtu_dbg[:, 32:64], offu2, inc="d_s")
            p.dma("sync", rtu_dbg[:, 64:96], offc1, inc="d_s")
            p.dma("sync", rtu_dbg[:, 96:128], offc2, inc="d_s")
            p.dma("sync", rtu_dbg[:, 128:160], idcolv, inc="d_s")
            p.dma("sync", rtu_dbg[:, 160:192], shidtv, inc="d_s")
            _finish(p, env, block)
            return nc

        entries4 = [("sh", b) for b in range(4)] + \
                   [("ex", e) for e in range(NEXP)]

        def load_w(which_w, src):
            wv = src.rearrange("(k p) h -> p k h", p=128)
            spl = W1SPL if which_w == 1 else W2SPL
            dst = w14 if which_w == 1 else w24
            for (k0, kn) in spl:
                p.dma("gpsimd", dst[:, k0:k0 + kn, :], wv[:, k0:k0 + kn, :])

        W1SPL = [(0, 4), (4, 4), (8, 4), (12, 4)]
        W2SPL = [(0, 3), (3, 3), (6, 3), (9, 2)]
        load_w(1, sw1[0])
        load_w(2, sw2[0])
        w1_ld = {0: p.cnt["d_g"]}
        w2_ld = {0: p.cnt["d_g"]}
        ia_b = 0
        ib_b = 0
        last_scatter = {}
        wm("gpsimd", f"p3spill{SB - 1}")
        def emit_gath(gi, cur):
            kind_, en_ = entries4[gi]
            is_sh_ = (kind_ == "sh")
            if not is_sh_:
                wm("gpsimd", "idx_done")
                if gi >= 2:
                    wm("gpsimd", f"bev{gi - 2}_{CAPJ - 1}")
                    wm("gpsimd", f"scat_fin{gi - 2}")
                p.dma("gpsimd", idxtv[gi % 2], idxs3[en_])
                p.dma("gpsimd", cvtv[gi % 2], cvals3[en_])
            idx_v = p.cnt["d_g"]
            if not is_sh_:
                p.wait("vector", "d_g", idx_v)
                if gi >= 2:
                    wm("vector", f"xet_done{gi - 2}")
                p.vcopy(gtmpv, idxtv[gi % 2], inc="m_dve")
                p.vfence()
                p.ts(gtmpv, gtmpv, float(S - 1), ALU.min, inc="m_dve")
                p.vfence()
                p.vcopy(gidxtv[gi % 2], gtmpv, inc="m_dve")
                mark(f"gidx{gi}", "m_dve")
            for jj in range(CAPJ):
                if jj >= 2:
                    wm("gpsimd", f"xtr{gi}_{jj - 2}")
                elif gi > 0:
                    wm("gpsimd", f"xtr{gi - 1}_{jj + CAPJ - 2}")
                if is_sh_:
                    icol = shidtv[:, en_ * 8 + jj:en_ * 8 + jj + 1]
                    p.wait("gpsimd", "d_g", idx_v)
                else:
                    icol = gidxtv[gi % 2][:, jj:jj + 1]
                    wm("gpsimd", f"gidx{gi}")
                p.op("gpsimd",
                     lambda h, env, o=xgv[jj % 2], ic=icol:
                     h.indirect_dma_start(
                         o, None, x1r_s, IndirectOffsetOnAxis(ic, 0)),
                     inc="d_g", inck=16)
                gv = p.cnt["d_g"]
                p.wait("tensor", "d_g", gv)
                for half in range(2):
                    bank = half
                    p.pe_bank(bank)
                    for t in range(8):
                        k = half * 8 + t
                        p.tr(ps_bf(bank, t),
                             xgv[jj % 2][:, k * 128:(k + 1) * 128], idv,
                             inc="m_pe" if t == 7 else None)
                    if half == 1:
                        mark(f"xtr{gi}_{jj}", "m_pe")
                    p.wait("scalar", "m_pe")
                    if cur is not None and jj == 0 and half == 0:
                        wm("scalar", f"A_done{cur}")
                    p.acopy(xet3[:, half * 8:half * 8 + 8,
                                 jj * 128:jj * 128 + 128],
                            ps_bf8(bank), inc="m_act")
                    p.bank_read(bank, "m_act")
            mark(f"xet_done{gi}", "m_act")

        emit_gath(0, None)
        for ei, (kind, en) in enumerate(entries4):
            is_sh = (kind == "sh")
            p.wait("tensor", "d_g", w1_ld[0 if is_sh else ei])
            wm("tensor", f"xet_done{ei}")
            for sc2 in range(2):
                for hb in range(KH):
                    bank = ia_b % 4
                    ia_b += 1
                    p.pe_bank(bank)
                    for k in range(KD):
                        p.mm(ps[bank][:, :], w14[:, k, hb * 128:(hb + 1) * 128],
                             xet3[:, k, sc2 * 512:(sc2 + 1) * 512],
                             start=(k == 0), stop=(k == KD - 1),
                             inc=("m_pe" if k == KD - 1 else None))
                    mark(f"Amm{ei}_{sc2}_{hb}", "m_pe")
                    wm("scalar", f"Amm{ei}_{sc2}_{hb}")
                    if ei > 0 and sc2 == 0 and hb == 0:
                        wm("scalar", f"Bmm_done{ei - 1}")
                    p.act(h4v[:, hb, sc2 * 512:(sc2 + 1) * 512],
                          ps[bank][:, :], AF.Silu, inc="m_act")
                    p.bank_read(bank, "m_act")
            mark(f"A_done{ei}", "m_pe")
            mark(f"silu_done{ei}", "m_act")
            mark(f"gath_done{ei}", "m_pe")
            if ei + 1 < len(entries4):
                emit_gath(ei + 1, ei)
            if not is_sh and ei + 0 >= 4:
                pass
            if ei >= 4 or (is_sh and en == 3):
                pass
            p.wait("tensor", "d_g", w2_ld[0 if is_sh else ei])
            wm("tensor", f"silu_done{ei}")
            for sbl in range(CAPJ):
                for ec in range(4):
                    bank = 4 + ib_b % 4
                    ib_b += 1
                    p.pe_bank(bank)
                    for k in range(KH):
                        p.mm(ps[bank][:, :],
                             h4v[:, k, sbl * 128:(sbl + 1) * 128],
                             w24[:, k, ec * 512:(ec + 1) * 512],
                             start=(k == 0), stop=(k == KH - 1),
                             inc=("m_pe" if k == KH - 1 else None))
                    mark(f"Bmm{ei}_{sbl}_{ec}", "m_pe")
                    wm("vector", f"Bmm{ei}_{sbl}_{ec}")
                    if sbl >= 2 and ec == 0:
                        wm("vector", f"scat{ei}_{sbl - 2}")
                    if sbl < 2 and ec == 0 and ei > 0:
                        wm("vector", f"scat{ei - 1}_{sbl + 6}")
                    ydst = yevv[sbl % 2][:, ec * 512:(ec + 1) * 512]
                    if is_sh:
                        p.vcopy(ydst, ps[bank][:, :],
                                inc="m_dve")
                    else:
                        p.ts(ydst, ps[bank][:, :],
                             cvtfv[ei % 2][:, sbl:sbl + 1], ALU.mult,
                             inc="m_dve")
                    p.bank_read(bank, "m_dve")
                mark(f"bev{ei}_{sbl}", "m_dve")
                wm("gpsimd", f"bev{ei}_{sbl}")
                if ei in last_scatter:
                    p.wait("gpsimd", "d_g", last_scatter[ei])
                elif ei > 0 and (ei - 1) in last_scatter:
                    pass
                if ei == 4 and sbl == 0:
                    wm("gpsimd", "shared_scat_done")
                if ei > 4 and sbl == 0:
                    wm("gpsimd", f"scat_fin{ei - 1}")
                if is_sh:
                    r0_ = en * 1024 + sbl * 128
                    p.dma("gpsimd", yrows[r0_:r0_ + 128, :], yevv[sbl % 2])
                else:
                    icol2 = idxtv[ei % 2][:, sbl:sbl + 1]
                    p.op("gpsimd",
                         lambda h, env, ic=icol2, src=yevv[sbl % 2]:
                         h.indirect_dma_start(
                             yrows, IndirectOffsetOnAxis(ic, 0), src, None,
                             compute_op=ALU.add),
                         inc="d_g", inck=16)
                mark(f"scat{ei}_{sbl}", "d_g")
            mark(f"Bmm_done{ei}", "m_pe")
            mark(f"scat_fin{ei}", "d_g")
            if is_sh and en == 3:
                mark("shared_scat_done", "d_g")
            nx = ei + 1
            if nx < len(entries4):
                nkind, nen = entries4[nx]
                if nkind == "ex":
                    wm("gpsimd", f"A_done{ei}")
                    load_w(1, w1[nen])
                    w1_ld[nx] = p.cnt["d_g"]
                    wm("gpsimd", f"Bmm_done{ei}")
                    load_w(2, w2[nen])
                    w2_ld[nx] = p.cnt["d_g"]
                else:
                    w1_ld[nx] = w1_ld[0]
                    w2_ld[nx] = w2_ld[0]

        all_scat = p.cnt["d_g"]
        wm("sync", f"scat_fin{len(entries4) - 1}")

        def fin_a(sbi):
            buf = sbi % 4
            if sbi >= 4:
                wm("gpsimd", f"p4out{sbi - 4}")
                wm("gpsimd", f"p4norm{sbi - 4}")
            if sbi == 0:
                p.wait("gpsimd", "d_g", all_scat)
            p.dma("gpsimd", finy[buf], yrows[sbi * 128:(sbi + 1) * 128, :])
            yl = p.cnt["d_g"]
            p.dma("sync", finx[buf], x1f_s[sbi], inc="d_s")
            xl = p.cnt["d_s"]
            p.wait("vector", "d_g", yl)
            p.wait("vector", "d_s", xl)
            ysb = finy[buf]
            p.tt(ysb, ysb, finx[buf], ALU.add, inc="m_dve")
            mark(f"p4add{sbi}", "m_dve")
            wm("scalar", f"p4add{sbi}")
            st4 = fst[buf]
            p.act(finx[buf], ysb, AF.Identity, accum=st4[:, 0:1], inc="m_act")
            p.act(finx[buf], ysb, AF.Square, accum=st4[:, 1:2], inc="m_act")
            mark(f"p4acc{sbi}", "m_act")

        def fin_b(sbi):
            buf = sbi % 4
            st4 = fst[buf]
            ysb = finy[buf]
            wm("vector", f"p4acc{sbi}")
            p.ts(st4[:, 2:3], st4[:, 0:1], 1.0 / D, ALU.mult, inc="m_dve")
            p.ts(st4[:, 3:4], st4[:, 1:2], 1.0 / D, ALU.mult, inc="m_dve")
            p.vfence()
            p.tt(st4[:, 4:5], st4[:, 2:3], st4[:, 2:3], ALU.mult,
                 inc="m_dve")
            p.vfence()
            p.tt(st4[:, 5:6], st4[:, 3:4], st4[:, 4:5], ALU.subtract,
                 inc="m_dve")
            p.vfence()
            mark(f"p4stB{sbi}", "m_dve")
            wm("scalar", f"p4stB{sbi}")
            p.act(st4[:, 6:7], st4[:, 5:6], AF.Ln, bias=cc(C_EPS),
                  inc="m_act")
            p.afence()
            p.act(st4[:, 7:8], st4[:, 6:7], AF.Exp, scale=cc(C_NEGHALF),
                  inc="m_act")
            mark(f"p4lnexp{sbi}", "m_act")
            wm("vector", f"p4lnexp{sbi}")
            p.ts(st4[:, 4:5], st4[:, 2:3], st4[:, 7:8], ALU.mult,
                 s2=-1.0, op1=ALU.mult, inc="m_dve")
            p.vfence()
            mark(f"p4stD{sbi}", "m_dve")
            wm("scalar", f"p4stD{sbi}")
            p.act(finx[buf], ysb, AF.Identity, scale=st4[:, 7:8],
                  bias=st4[:, 4:5], inc="m_act")
            mark(f"p4norm{sbi}", "m_act")
            wm("sync", f"p4norm{sbi}")
            p.dma("sync", out_t[sbi * 128:(sbi + 1) * 128, :], finx[buf],
                  inc="d_s")
            mark(f"p4out{sbi}", "d_s")

        for i in range(4):
            fin_a(i)
        for sbi in range(SB):
            fin_b(sbi)
            if sbi + 4 < SB:
                fin_a(sbi + 4)


        _finish(p, env, block)
        return nc


def _finish(p, env, block):
    p.wait("sync", "d_s")
    p.wait("sync", "d_g")
    p.build(block, env)


def kernel(**inputs):
    x_full = np.ascontiguousarray(inputs["L0_attn_x"], dtype=np.float32)
    base = {
        "W_qkv": np.ascontiguousarray(inputs["L0_attn_W_qkv"], dtype=np.float32),
        "W_out": np.ascontiguousarray(inputs["L0_attn_W_out"], dtype=np.float32),
        "gate_w": np.ascontiguousarray(inputs["L0_ffn_gate_w"], dtype=np.float32),
        "w1": np.ascontiguousarray(inputs["L0_ffn_w1"], dtype=np.float32),
        "w2": np.ascontiguousarray(inputs["L0_ffn_w2"], dtype=np.float32),
        "sw1": np.ascontiguousarray(inputs["L0_ffn_sw1"], dtype=np.float32),
        "sw2": np.ascontiguousarray(inputs["L0_ffn_sw2"], dtype=np.float32),
    }
    nc = build_program(stage=4)
    in_maps = [dict(base, x=np.ascontiguousarray(x_full[b])) for b in range(8)]
    res = run_bass_kernel_spmd(nc, in_maps, core_ids=list(range(8)))
    return np.stack([np.asarray(res.results[b]["out"]) for b in range(8)],
                    axis=0)
